# version 15
# baseline (speedup 1.0000x reference)
import numpy as np
from contextlib import ExitStack
import concourse.bass as bass
import concourse.mybir as mybir
from concourse.bass_utils import run_bass_kernel_spmd

F32 = mybir.dt.float32
I32 = mybir.dt.int32
AF = mybir.ActivationFunctionType
ALU = mybir.AluOpType
AX = mybir.AxisListType.X

NCORE = 8
B, T, DM = 4, 4096, 1024
NTOK = B * T
TPC = NTOK // NCORE
NTILE = TPC // 128
H, D, S = 16, 64, 64
E, FF = 16, 2048
CAP = 256
NSLOT = E * CAP
TRASH = NSLOT
W = 512
NW = T // W
LEPS = 1e-5
REPS = 1e-4
L3_PH = 5


class Sem:
    def __init__(self, h):
        self.h = h
        self.n = 0

    def inc(self, inst, k=1):
        if k:
            inst.then_inc(self.h, k)
        self.n += k
        return self.n


def build_l1():
    nc = bass.Bass("TRN2", target_bir_lowering=False, debug=False, detect_race_conditions=False)
    x_h = nc.dram_tensor("x", [TPC, DM], F32, kind="ExternalInput").ap()
    nwrep_h = nc.dram_tensor("nwrep", [128, DM], F32, kind="ExternalInput").ap()
    wrt_h = nc.dram_tensor("wrt", [DM, E], F32, kind="ExternalInput").ap()
    ident_h = nc.dram_tensor("ident", [128, 128], F32, kind="ExternalInput").ap()
    ltri_h = nc.dram_tensor("ltri", [128, 128], F32, kind="ExternalInput").ap()
    iota_h = nc.dram_tensor("iota", [128, E], F32, kind="ExternalInput").ap()
    iota99_h = nc.dram_tensor("iota99", [128, E], F32, kind="ExternalInput").ap()
    ones_h = nc.dram_tensor("onesc", [128, 128], F32, kind="ExternalInput").ap()

    xn_h = nc.dram_tensor("xn", [TPC, DM], F32, kind="ExternalOutput").ap()
    xnT_h = nc.dram_tensor("xnT", [8, 128, TPC], F32, kind="ExternalOutput").ap()
    x2T_h = nc.dram_tensor("x2T", [8, 128, TPC], F32, kind="ExternalOutput").ap()
    dest_h = nc.dram_tensor("dest", [128, NTILE], I32, kind="ExternalOutput").ap()
    score_h = nc.dram_tensor("score", [128, NTILE], F32, kind="ExternalOutput").ap()

    es = ExitStack()
    _E = es.enter_context
    nwrep = _E(nc.sbuf_tensor("nwrep_sb", [128, DM], F32))
    wrt = _E(nc.sbuf_tensor("wrt_sb", [128, 8, E], F32))
    ident = _E(nc.sbuf_tensor("ident_sb", [128, 128], F32))
    ltri = _E(nc.sbuf_tensor("ltri_sb", [128, 128], F32))
    iota = _E(nc.sbuf_tensor("iota_sb", [128, E], F32))
    iota99 = _E(nc.sbuf_tensor("iota99_sb", [128, E], F32))
    onesc = _E(nc.sbuf_tensor("onesc_sb", [128, 128], F32))
    xt = _E(nc.sbuf_tensor("xt_sb", [128, 2, DM], F32))
    xn_t = _E(nc.sbuf_tensor("xn_t_sb", [128, DM], F32))
    x2_t = _E(nc.sbuf_tensor("x2_t_sb", [128, DM], F32))
    sq = _E(nc.sbuf_tensor("sq_sb", [128, DM], F32))
    xnT_t = _E(nc.sbuf_tensor("xnT_t_sb", [128, 8, 128], F32))
    x2T_t = _E(nc.sbuf_tensor("x2T_t_sb", [128, 8, 128], F32))
    cols = _E(nc.sbuf_tensor("cols_sb", [128, 16], F32))
    sig = _E(nc.sbuf_tensor("sig_sb", [128, E], F32))
    mask = _E(nc.sbuf_tensor("mask_sb", [128, E], F32))
    tmpe = _E(nc.sbuf_tensor("tmpe_sb", [128, E], F32))
    prefix = _E(nc.sbuf_tensor("prefix_sb", [1, E], F32))
    dst_i = _E(nc.sbuf_tensor("dst_i_sb", [128, NTILE], I32))
    sc_s = _E(nc.sbuf_tensor("sc_s_sb", [128, NTILE], F32))
    epsc = _E(nc.sbuf_tensor("epsc_sb", [128, 2], F32))
    psT0 = _E(nc.psum_tensor("psT0", [128, 512], F32))
    psT1 = _E(nc.psum_tensor("psT1", [128, 512], F32))
    psL = _E(nc.psum_tensor("psL", [128, 512], F32))
    psR = _E(nc.psum_tensor("psR", [128, 512], F32))
    psC = _E(nc.psum_tensor("psC", [128, 512], F32))
    _ld = _E(nc.semaphore("ld"))
    _vs = _E(nc.semaphore("vs"))
    _ss = _E(nc.semaphore("ss"))
    _ts = _E(nc.semaphore("ts"))
    _st = _E(nc.semaphore("st"))
    blk = _E(nc.Block())
    if True:
        ld, vs, ss, ts, st = Sem(_ld), Sem(_vs), Sem(_ss), Sem(_ts), Sem(_st)
        psT = [psT0, psT1]

        def _(sy):
            for dst, src in [(nwrep, nwrep_h), (ident, ident_h), (ltri, ltri_h),
                             (iota, iota_h), (iota99, iota99_h), (onesc, ones_h)]:
                ld.inc(sy.dma_start(out=dst[:], in_=src[:, :]), 16)
            ld.inc(sy.dma_start(out=wrt[:, :, :],
                                in_=wrt_h.rearrange("(c p) e -> p c e", p=128)), 16)
        blk.sync(_)

        def _(ve):
            ve.memset(prefix[:], 0.0)
            ve.memset(epsc[:, 0:1], LEPS)
            vs.inc(ve.memset(epsc[:, 1:2], REPS))
        blk.vector(_)

        last_ts = 0
        last_ss = 0
        last_st = 0
        xt_rd = {}
        trc = []

        for i in range(NTILE):
            sl = i % 2

            def _(sy, i=i, sl=sl, w=xt_rd.get(i - 2, 0)):
                if w:
                    sy.wait_ge(vs.h, w)
                ld.inc(sy.dma_start(out=xt[:, sl],
                                    in_=x_h[i * 128:(i + 1) * 128, :]), 16)
            blk.sync(_)
            x_ld = ld.n

            def _(ve, sl=sl, x_ld=x_ld, a=last_ts, b=last_ss, c=last_st):
                ve.wait_ge(ld.h, x_ld)
                if a:
                    ve.wait_ge(ts.h, a)
                if b:
                    ve.wait_ge(ss.h, b)
                if c:
                    ve.wait_ge(st.h, c)
                ve.reduce_sum(cols[:, 0:1], xt[:, sl], axis=AX)
                ve.drain()
                ve.tensor_scalar(cols[:, 1:2], cols[:, 0:1], 1.0 / DM, None,
                                 op0=ALU.mult)
                ve.drain()
                vs.inc(ve.tensor_scalar(xn_t[:, :], xt[:, sl], cols[:, 1:2], None,
                                        op0=ALU.subtract))
            blk.vector(_)
            xt_rd[i] = vs.n
            xm_done = vs.n

            def _(sc, n=xm_done):
                sc.wait_ge(vs.h, n)
                ss.inc(sc.activation(sq[:, :], xn_t[:, :], AF.Square))
            blk.scalar(_)
            sq1 = ss.n

            def _(ve, n=sq1):
                ve.wait_ge(ss.h, n)
                ve.reduce_sum(cols[:, 2:3], sq[:, :], axis=AX)
                ve.drain()
                vs.inc(ve.tensor_scalar(cols[:, 3:4], cols[:, 2:3], 1.0 / DM, None,
                                        op0=ALU.mult))
            blk.vector(_)
            var_done = vs.n

            def _(sc, n=var_done):
                sc.wait_ge(vs.h, n)
                ss.inc(sc.activation(cols[:, 4:5], cols[:, 3:4], AF.Sqrt, bias=epsc[:, 0:1]))
            blk.scalar(_)
            std_done = ss.n

            def _(ve, n=std_done):
                ve.wait_ge(ss.h, n)
                ve.reciprocal(cols[:, 5:6], cols[:, 4:5])
                ve.drain()
                vs.inc(ve.tensor_scalar(xn_t[:, :], xn_t[:, :], cols[:, 5:6], None,
                                        op0=ALU.mult))
            blk.vector(_)
            xn_done = vs.n

            def _(sy, i=i, n=xn_done):
                sy.wait_ge(vs.h, n)
                st.inc(sy.dma_start(out=xn_h[i * 128:(i + 1) * 128, :],
                                    in_=xn_t[:, :]), 16)
            blk.sync(_)

            def _(sc, n=xn_done):
                sc.wait_ge(vs.h, n)
                ss.inc(sc.activation(sq[:, :], xn_t[:, :], AF.Square))
            blk.scalar(_)
            sq2 = ss.n

            def _(ve, n=sq2):
                ve.wait_ge(ss.h, n)
                ve.reduce_sum(cols[:, 6:7], sq[:, :], axis=AX)
                ve.drain()
                vs.inc(ve.tensor_scalar(cols[:, 7:8], cols[:, 6:7], 1.0 / DM, None,
                                        op0=ALU.mult))
            blk.vector(_)
            ms_done = vs.n

            def _(sc, n=ms_done):
                sc.wait_ge(vs.h, n)
                ss.inc(sc.activation(cols[:, 8:9], cols[:, 7:8], AF.Sqrt, bias=epsc[:, 1:2]))
            blk.scalar(_)
            rstd_done = ss.n

            def _(ve, n=rstd_done):
                ve.wait_ge(ss.h, n)
                ve.reciprocal(cols[:, 9:10], cols[:, 8:9])
                ve.drain()
                vs.inc(ve.scalar_tensor_tensor(x2_t[:, :], xn_t[:, :], cols[:, 9:10],
                                               nwrep[:], op0=ALU.mult, op1=ALU.mult))
            blk.vector(_)
            x2_done = vs.n

            for c in range(16):
                cc = c % 8
                slot = len(trc) % 2
                guard = trc[-2] if len(trc) >= 2 else 0

                def _(te, c=c, cc=cc, slot=slot, guard=guard,
                      xn_n=xn_done, x2_n=x2_done):
                    te.wait_ge(vs.h, x2_n if c >= 8 else xn_n)
                    if guard:
                        te.wait_ge(vs.h, guard)
                    src = xn_t if c < 8 else x2_t
                    ts.inc(te.transpose(psT[slot][:, 0:128],
                                        src[:, cc * 128:(cc + 1) * 128], ident[:]))
                blk.tensor(_)
                tr_n = ts.n

                def _(ve, c=c, cc=cc, slot=slot, n=tr_n):
                    ve.wait_ge(ts.h, n)
                    dst = xnT_t[:, cc] if c < 8 else x2T_t[:, cc]
                    vs.inc(ve.tensor_copy(dst, psT[slot][:, 0:128]))
                blk.vector(_)
                trc.append(vs.n)
            xnT_done = trc[-9]
            x2T_done = trc[-1]

            def _(sy, i=i, a=xnT_done, b=x2T_done):
                sy.wait_ge(vs.h, a)
                st.inc(sy.dma_start(
                    out=xnT_h[:, :, i * 128:(i + 1) * 128]
                    .rearrange("c p t -> p c t"), in_=xnT_t[:, :, :]), 16)
                sy.wait_ge(vs.h, b)
                st.inc(sy.dma_start(
                    out=x2T_h[:, :, i * 128:(i + 1) * 128]
                    .rearrange("c p t -> p c t"), in_=x2T_t[:, :, :]), 16)
            blk.sync(_)

            def _(te, n=xnT_done):
                te.wait_ge(vs.h, n)
                for c in range(8):
                    mm = te.matmul(psL[:, 0:E], xnT_t[:, c], wrt[:, c, :],
                                   start=(c == 0), stop=(c == 7))
                ts.inc(mm)
            blk.tensor(_)
            logits_done = ts.n

            def _(sc, n=logits_done):
                sc.wait_ge(ts.h, n)
                ss.inc(sc.activation(sig[:, :], psL[:, 0:E], AF.Sigmoid))
            blk.scalar(_)
            sig_done = ss.n

            def _(ve, i=i, n=logits_done, m=sig_done):
                ve.wait_ge(ts.h, n)
                ve.wait_ge(ss.h, m)
                ve.reduce_max(cols[:, 10:11], psL[:, 0:E], axis=AX)
                ve.drain()
                ve.tensor_scalar(mask[:, :], psL[:, 0:E], cols[:, 10:11], None,
                                 op0=ALU.is_ge)
                ve.drain()
                ve.tensor_tensor(tmpe[:, :], mask[:, :], iota99[:, :], op=ALU.mult)
                ve.drain()
                ve.tensor_scalar(tmpe[:, :], tmpe[:, :], 99.0, None, op0=ALU.add)
                ve.drain()
                ve.tensor_reduce(cols[:, 11:12], tmpe[:, :], axis=AX, op=ALU.min)
                ve.drain()
                ve.tensor_scalar(mask[:, :], iota[:, :], cols[:, 11:12], None,
                                 op0=ALU.is_equal)
                ve.drain()
                ve.tensor_tensor(tmpe[:, :], sig[:, :], mask[:, :], op=ALU.mult)
                ve.drain()
                vs.inc(ve.reduce_sum(sc_s[:, i:i + 1], tmpe[:, :], axis=AX))
            blk.vector(_)
            mask_done = vs.n

            def _(te, n=mask_done):
                te.wait_ge(vs.h, n)
                te.matmul(psR[:, 0:E], ltri[:], mask[:, :], start=True, stop=False)
                te.matmul(psR[:, 0:E], onesc[0:1, :], prefix[:, :],
                          start=False, stop=True)
                ts.inc(te.matmul(psC[0:1, 0:E], onesc[:, 0:1], mask[:, :],
                                 start=True, stop=True))
            blk.tensor(_)
            rank_done = ts.n

            def _(ve, i=i, n=rank_done):
                ve.wait_ge(ts.h, n)
                ve.tensor_tensor(tmpe[:, :], psR[:, 0:E], mask[:, :], op=ALU.mult)
                ve.drain()
                ve.reduce_sum(cols[:, 12:13], tmpe[:, :], axis=AX)
                ve.tensor_tensor(prefix[:, :], prefix[:, :], psC[0:1, 0:E],
                                 op=ALU.add)
                ve.tensor_scalar(cols[:, 13:14], cols[:, 11:12], float(CAP), None,
                                 op0=ALU.mult)
                ve.drain()
                ve.tensor_tensor(cols[:, 13:14], cols[:, 13:14], cols[:, 12:13],
                                 op=ALU.add)
                ve.drain()
                ve.tensor_scalar(cols[:, 13:14], cols[:, 13:14], float(TRASH), None,
                                 op0=ALU.min)
                ve.drain()
                vs.inc(ve.tensor_copy(dst_i[:, i:i + 1], cols[:, 13:14]))
            blk.vector(_)

            last_ts = rank_done
            last_ss = sig_done
            last_st = st.n

        def _(sy, n=vs.n):
            sy.wait_ge(vs.h, n)
            st.inc(sy.dma_start(out=dest_h[:, :], in_=dst_i[:, :]), 16)
            st.inc(sy.dma_start(out=score_h[:, :], in_=sc_s[:, :]), 16)
            sy.wait_ge(st.h, st.n)
        blk.sync(_)
    es.close()
    return nc


def build_l2():
    nc = bass.Bass("TRN2", target_bir_lowering=False, debug=False, detect_race_conditions=False)
    a2_h = nc.dram_tensor("a2", [2, 64, 64], F32, kind="ExternalInput").ap()
    b2_h = nc.dram_tensor("b2", [2, 64, 64], F32, kind="ExternalInput").ap()
    x2w_h = nc.dram_tensor("x2w", [NW, 128, 4 * W], F32, kind="ExternalInput").ap()
    st_h = nc.dram_tensor("states", [NW, 128, 4 * W], F32, kind="ExternalOutput").ap()

    NB = 4
    NG = T // 128

    es = ExitStack()
    _E = es.enter_context
    Ablk = _E(nc.sbuf_tensor("Ablk_sb", [128, 128], F32))
    Bblk = _E(nc.sbuf_tensor("Bblk_sb", [128, 128], F32))
    ascr = _E(nc.sbuf_tensor("ascr_sb", [128, 64], F32))
    x2w = _E(nc.sbuf_tensor("x2w_sb", [128, 2, 4 * W], F32))
    stw = _E(nc.sbuf_tensor("stw_sb", [128, 2, 4 * W], F32))
    st0 = _E(nc.sbuf_tensor("st0_sb", [128, 4], F32))
    psU0 = _E(nc.psum_tensor("psU0", [128, 512], F32))
    psU1 = _E(nc.psum_tensor("psU1", [128, 512], F32))
    psU2 = _E(nc.psum_tensor("psU2", [128, 512], F32))
    psU3 = _E(nc.psum_tensor("psU3", [128, 512], F32))
    _ld = _E(nc.semaphore("ld"))
    _vs = _E(nc.semaphore("vs"))
    _uS = _E(nc.semaphore("uS"))
    _mm = _E(nc.semaphore("mmS"))
    _th = _E(nc.semaphore("tanhS"))
    _st = _E(nc.semaphore("stS"))
    blk = _E(nc.Block())
    if True:
        psU = [psU0, psU1, psU2, psU3]

        @blk.vector
        def _(ve):
            ve.memset(st0[:], 0.0)
            ve.memset(Ablk[:], 0.0)
            ve.memset(Bblk[:], 0.0).then_inc(_vs, 1)

        @blk.sync
        def _(sy):
            sy.wait_ge(_vs, 1)
            sy.dma_start(out=ascr[0:64, :], in_=a2_h[0]).then_inc(_ld, 16)
            sy.dma_start(out=ascr[64:128, :], in_=a2_h[1]).then_inc(_ld, 16)
            sy.dma_start(out=Bblk[0:64, 0:64], in_=b2_h[0]).then_inc(_ld, 16)
            sy.dma_start(out=Bblk[64:128, 64:128], in_=b2_h[1]).then_inc(_ld, 16)
            sy.dma_start(out=x2w[:, 0], in_=x2w_h[0]).then_inc(_ld, 16)
            sy.dma_start(out=x2w[:, 1], in_=x2w_h[1]).then_inc(_ld, 16)
            for w in range(NW):
                if w + 2 < NW:
                    sy.wait_ge(_uS, 4 * (w + 1))
                    sy.dma_start(out=x2w[:, (w + 2) % 2],
                                 in_=x2w_h[w + 2]).then_inc(_ld, 16)
                sy.wait_ge(_th, 2 + (w + 1) * W)
                sy.dma_start(out=st_h[w], in_=stw[:, w % 2]).then_inc(_st, 16)
            sy.wait_ge(_st, 16 * NW)

        @blk.scalar
        def _(sc):
            sc.wait_ge(_ld, 32)
            sc.activation(Ablk[0:64, 0:64], ascr[0:64, :], AF.Tanh).then_inc(_th, 1)
            sc.activation(Ablk[64:128, 64:128], ascr[64:128, :],
                          AF.Tanh).then_inc(_th, 1)

        def umm(te, g):
            te.wait_ge(_ld, 64 + 16 * (g // 4 + 1))
            te.matmul(psU[g % NB][:, 0:512], Bblk[:],
                      x2w[:, (g // 4) % 2, (g % 4) * 512:(g % 4 + 1) * 512],
                      start=True, stop=False,
                      skip_group_check=True).then_inc(_uS, 1)

        @blk.tensor
        def _(te):
            te.wait_ge(_th, 2)
            te.wait_ge(_ld, 64)
            for g in range(3):
                umm(te, g)
            for t in range(T):
                g = t // 128
                if t > 0:
                    te.wait_ge(_th, 2 + t)
                if t % 128 == 0 and g + 3 < NG:
                    umm(te, g + 3)
                prev = st0[:, :] if t == 0 else \
                    stw[:, ((t - 1) // W) % 2,
                        ((t - 1) % W) * 4:((t - 1) % W) * 4 + 4]
                te.matmul(psU[g % NB][:, (t % 128) * 4:(t % 128) * 4 + 4],
                          Ablk[:], prev,
                          start=False, stop=(t % 128 == 127),
                          skip_group_check=True).then_inc(_mm, 1)
            te.wait_ge(_th, 2 + T)

        @blk.scalar
        def _(sc):
            for t in range(T):
                w = t // W
                if t % W == 0 and w >= 2:
                    sc.wait_ge(_st, 16 * (w - 1))
                sc.wait_ge(_mm, t + 1)
                sc.activation(stw[:, w % 2, (t % W) * 4:(t % W) * 4 + 4],
                              psU[(t // 128) % NB][:, (t % 128) * 4:(t % 128) * 4 + 4],
                              AF.Tanh).then_inc(_th, 1)

    es.close()
    return nc


def build_l3():
    nc = bass.Bass("TRN2", target_bir_lowering=False, debug=False, detect_race_conditions=False)
    xn_h = nc.dram_tensor("xn", [TPC + 1, 1088], F32, kind="ExternalInput").ap()
    xnT_h = nc.dram_tensor("xnT", [8, 128, TPC], F32, kind="ExternalInput").ap()
    x2T_h = nc.dram_tensor("x2T", [8, 128, TPC], F32, kind="ExternalInput").ap()
    x_h = nc.dram_tensor("x", [TPC, DM], F32, kind="ExternalInput").ap()
    stT_h = nc.dram_tensor("stT", [8, 128, TPC], F32, kind="ExternalInput").ap()
    dest_h = nc.dram_tensor("dest", [128, NTILE], I32, kind="ExternalInput").ap()
    tokid_h = nc.dram_tensor("tokid", [128, NTILE], I32, kind="ExternalInput").ap()
    gup_h = nc.dram_tensor("gup", [E, DM, 2 * FF], F32, kind="ExternalInput").ap()
    dwn_h = nc.dram_tensor("dwn", [E, FF, DM], F32, kind="ExternalInput").ap()
    wgT_h = nc.dram_tensor("wgT", [DM, FF], F32, kind="ExternalInput").ap()
    wuT_h = nc.dram_tensor("wuT", [DM, FF], F32, kind="ExternalInput").ap()
    wdT_h = nc.dram_tensor("wdT", [FF, DM], F32, kind="ExternalInput").ap()
    cT_h = nc.dram_tensor("cT", [H, D, S], F32, kind="ExternalInput").ap()
    woT_h = nc.dram_tensor("woT", [DM, DM], F32, kind="ExternalInput").ap()
    skip_h = nc.dram_tensor("skipc", [128, 8], F32, kind="ExternalInput").ap()
    bout_h = nc.dram_tensor("bout", [1, DM], F32, kind="ExternalInput").ap()
    ident_h = nc.dram_tensor("ident", [128, 128], F32, kind="ExternalInput").ap()
    ones_h = nc.dram_tensor("onesc", [128, 128], F32, kind="ExternalInput").ap()
    fill_h = nc.dram_tensor("fillc", [128, 2112], I32, kind="ExternalInput").ap()

    out_h = nc.dram_tensor("out", [TPC, DM], F32, kind="ExternalOutput").ap()

    srctok = nc.dram_tensor("srctok", [128 * 2112, 1], I32)
    y_hbm = nc.dram_tensor("y_hbm", [NSLOT + 1, DM], F32)
    v_hbm = nc.dram_tensor("v_hbm", [DM, DM], F32)

    es0 = ExitStack()
    _E = es0.enter_context
    ident = _E(nc.sbuf_tensor("ident_sb", [128, 128], F32))
    onesc = _E(nc.sbuf_tensor("onesc_sb", [128, 128], F32))
    dst_i = _E(nc.sbuf_tensor("dst_i_sb", [128, NTILE], I32))
    tokid = _E(nc.sbuf_tensor("tokid_sb", [128, NTILE], I32))
    fill = _E(nc.sbuf_tensor("fill_sb", [128, 2112], I32))
    dst16 = _E(nc.sbuf_tensor("dst16_sb", [128, NTILE], I32))
    bout = _E(nc.sbuf_tensor("bout_sb", [1, DM], F32))
    zrow = _E(nc.sbuf_tensor("zrow_sb", [1, DM], F32))
    _ld = _E(nc.semaphore("ld"))
    _vs = _E(nc.semaphore("vs"))
    _ss = _E(nc.semaphore("ss"))
    _ts = _E(nc.semaphore("ts"))
    _gp = _E(nc.semaphore("gp"))
    _st = _E(nc.semaphore("st"))
    _gw = _E(nc.semaphore("gw"))
    _gf = _E(nc.semaphore("gfree"))
    _dw = _E(nc.semaphore("dw"))
    _df = _E(nc.semaphore("dfree"))
    blk = _E(nc.Block())
    if True:
        ld, vs, ss, ts, gp, st = (Sem(_ld), Sem(_vs), Sem(_ss), Sem(_ts),
                                  Sem(_gp), Sem(_st))

        def _(sy):
            for dst, src in [(ident, ident_h), (onesc, ones_h)]:
                ld.inc(sy.dma_start(out=dst[:], in_=src[:, :]), 16)
            ld.inc(sy.dma_start(out=dst_i[:], in_=dest_h[:, :]), 16)
            ld.inc(sy.dma_start(out=tokid[:], in_=tokid_h[:, :]), 16)
            ld.inc(sy.dma_start(out=fill[:], in_=fill_h[:, :]), 16)
            ld.inc(sy.dma_start(out=bout[:], in_=bout_h[:, :]), 16)
        blk.sync(_)
        setup_ld = ld.n

        def _(ve, n=setup_ld):
            ve.memset(zrow[:], 0.0)
            ve.wait_ge(ld.h, n)
            vs.inc(ve.tensor_scalar(dst16[:, :], dst_i[:, :], 64, None,
                                    op0=ALU.mult))
        blk.vector(_)
        zrow_done = vs.n

        def _(g, n=setup_ld, z=zrow_done):
            g.wait_ge(ld.h, n)
            g.wait_ge(vs.h, z)
            gp.inc(g.dma_start(
                out=srctok.ap().rearrange("(p a) one -> p (a one)", p=128),
                in_=fill[:, :]), 16)
            g.wait_ge(gp.h, gp.n)
            for i in range(NTILE):
                gp.inc(g.indirect_dma_start(
                    out=srctok[:, :],
                    out_offset=bass.IndirectOffsetOnAxis(dst16[:, i:i + 1], 0),
                    in_=tokid[:, i:i + 1], in_offset=None), 16)
            gp.inc(g.dma_start(out=y_hbm[TRASH:TRASH + 1, :], in_=zrow[:, :]), 16)
            g.wait_ge(gp.h, gp.n)
        blk.gpsimd(_)
        srctok_done = gp.n

        es1 = ExitStack()
        _E = es1.enter_context
        idxs = _E(nc.sbuf_tensor("idxs_sb", [128, 2, 1], I32))
        scg = _E(nc.sbuf_tensor("scg_sb", [128, 2, 1], F32))
        Xg = _E(nc.sbuf_tensor("Xg_sb", [128, 2, 1088], F32))
        Xs = _E(nc.sbuf_tensor("Xs_sb", [128, 2, DM], F32))
        XcT = _E(nc.sbuf_tensor("XcT_sb", [128, 8, CAP], F32))
        Gbuf = _E(nc.sbuf_tensor("Gbuf_sb", [128, 2, 16, 512], F32))
        Dbuf = _E(nc.sbuf_tensor("Dbuf_sb", [128, 16, 512], F32))
        he = _E(nc.sbuf_tensor("he_sb", [128, 2, FF], F32))
        hT = _E(nc.sbuf_tensor("hT_sb", [128, 16, CAP], F32))
        ysb = _E(nc.sbuf_tensor("ysb_sb", [128, 2, DM], F32))
        silu_t = _E(nc.sbuf_tensor("silu_t_sb", [128, 512], F32))
        psA = _E(nc.psum_tensor("psA", [128, 512], F32))
        psB = _E(nc.psum_tensor("psB", [128, 512], F32))
        psDn = _E(nc.psum_tensor("psDn", [128, 512], F32))
        psTr0 = _E(nc.psum_tensor("psTr0", [128, 512], F32))
        psTr1 = _E(nc.psum_tensor("psTr1", [128, 512], F32))
        if True:
            psTr = [psTr0, psTr1]
            trc = []

            def _(sy):
                for e in range(E):
                    for j in range(4):
                        pj = e * 4 + j
                        if pj >= 2:
                            sy.wait_ge(_gf, pj - 1)
                        for dk in range(8):
                            sy.dma_start(out=Gbuf[:, pj % 2, dk, :],
                                         in_=gup_h[e, dk * 128:(dk + 1) * 128,
                                                   j * 512:(j + 1) * 512]
                                         ).then_inc(_gw, 16)
                        for dk in range(8):
                            sy.dma_start(
                                out=Gbuf[:, pj % 2, 8 + dk, :],
                                in_=gup_h[e, dk * 128:(dk + 1) * 128,
                                          FF + j * 512:FF + (j + 1) * 512]
                            ).then_inc(_gw, 16)
            blk.sync(_)

            gf_cnt = [0]
            df_cnt = [0]
            he_free_guard = [0]
            xg_guard = [0, 0]
            xs_guard = [0, 0]
            he_guard = {}
            ysb_guard = 0

            for e in range(E if L3_PH >= 2 else 0):
                for sg in range(2):
                    def _(g, e=e, sg=sg, n=srctok_done, w=xg_guard[sg]):
                        g.wait_ge(gp.h, n)
                        if w:
                            g.wait_ge(vs.h, w)
                        gp.inc(g.dma_start(
                            out=idxs[:, sg],
                            in_=srctok.ap().rearrange(
                                "(s w) one -> s (w one)", w=64)[
                                e * CAP + sg * 128:
                                e * CAP + (sg + 1) * 128, 0:1],
                            allow_slow_non_contiguous=True), 16)
                        g.wait_ge(gp.h, gp.n)
                        gp.inc(g.indirect_dma_start(
                            out=Xg[:, sg, 0:1088], out_offset=None,
                            in_=xn_h[:, :],
                            in_offset=bass.IndirectOffsetOnAxis(idxs[:, sg], 0)), 16)
                    blk.gpsimd(_)
                    gath_n = gp.n

                    def _(ve, sg=sg, n=gath_n, tw=xs_guard[sg]):
                        ve.wait_ge(gp.h, n)
                        if tw:
                            ve.wait_ge(ts.h, tw)
                        vs.inc(ve.tensor_scalar(Xs[:, sg],
                                                Xg[:, sg, 1:DM + 1],
                                                Xg[:, sg, 0:1],
                                                None, op0=ALU.mult))
                    blk.vector(_)
                    xs_done = vs.n

                    for c in range(8):
                        slot = len(trc) % 2
                        guard = trc[-2] if len(trc) >= 2 else 0

                        def _(te, sg=sg, c=c, slot=slot, guard=guard, n=xs_done):
                            te.wait_ge(vs.h, n)
                            if guard:
                                te.wait_ge(vs.h, guard)
                            ts.inc(te.transpose(psTr[slot][:, 0:128],
                                                Xs[:, sg, c * 128:(c + 1) * 128],
                                                ident[:]))
                        blk.tensor(_)
                        tr_n = ts.n

                        def _(ve, sg=sg, c=c, slot=slot, n=tr_n):
                            ve.wait_ge(ts.h, n)
                            vs.inc(ve.tensor_copy(
                                XcT[:, c, sg * 128:(sg + 1) * 128],
                                psTr[slot][:, 0:128]))
                        blk.vector(_)
                        trc.append(vs.n)
                    xg_guard[sg] = xs_done
                    xs_guard[sg] = ts.n
                xct_ready = trc[-1]

                def _(g, e=e):
                    for nh in range(2):
                        q = e * 2 + nh
                        if q >= 1:
                            g.wait_ge(_df, q)
                        for fk in range(16):
                            g.dma_start(
                                out=Dbuf[:, fk, :],
                                in_=dwn_h[e, fk * 128:(fk + 1) * 128,
                                          nh * 512:(nh + 1) * 512]
                            ).then_inc(_dw, 16)
                blk.gpsimd(_)

                silu_rd = 0
                he_rd = {}
                for j in range(4):
                    pj = e * 4 + j
                    for sg in range(2):
                        def _(te, pj=pj, j=j, sg=sg, n=xct_ready,
                              sr=silu_rd,
                              hr=he_rd.get((j, sg), 0)):
                            te.wait_ge(vs.h, n)
                            te.wait_ge(_gw, 256 * (pj + 1))
                            if sr:
                                te.wait_ge(ss.h, sr)
                            if hr:
                                te.wait_ge(vs.h, hr)
                            for dk in range(8):
                                mm = te.matmul(psA[:, 0:512],
                                               XcT[:, dk, sg * 128:(sg + 1) * 128],
                                               Gbuf[:, pj % 2, dk, :],
                                               start=(dk == 0), stop=(dk == 7))
                            mm.then_inc(_ts, 1)
                            for dk in range(8):
                                mm = te.matmul(psB[:, 0:512],
                                               XcT[:, dk, sg * 128:(sg + 1) * 128],
                                               Gbuf[:, pj % 2, 8 + dk, :],
                                               start=(dk == 0), stop=(dk == 7))
                            mm.then_inc(_gf if sg == 1 else _ts, 1)
                        blk.tensor(_)
                        ts.n += 1
                        gu_mm = ts.n
                        if sg == 0:
                            ts.n += 1
                            psb_wait = ("ts", ts.n)
                        else:
                            gf_cnt[0] += 1
                            psb_wait = ("gf", gf_cnt[0])

                        def _(sc, n=gu_mm, hf=he_free_guard[0]):
                            sc.wait_ge(ts.h, n)
                            if hf:
                                sc.wait_ge(vs.h, hf)
                            ss.inc(sc.activation(silu_t[:, :], psA[:, 0:512],
                                                 AF.Silu))
                        blk.scalar(_)
                        silu_rd = ss.n

                        def _(ve, sg=sg, j=j, n=silu_rd,
                              hg=he_guard.get((sg, j), 0), pw=psb_wait):
                            ve.wait_ge(ss.h, n)
                            if hg:
                                ve.wait_ge(ts.h, hg)
                            ve.wait_ge(ts.h if pw[0] == "ts" else _gf, pw[1])
                            vs.inc(ve.tensor_tensor(
                                he[:, sg, j * 512:(j + 1) * 512],
                                silu_t[:, :], psB[:, 0:512], op=ALU.mult))
                        blk.vector(_)
                        he_rd[(j, sg)] = vs.n
                        he_free_guard[0] = vs.n
                he_all = vs.n

                for sg in range(2):
                    for fk in range(16):
                        slot = len(trc) % 2
                        guard = trc[-2] if len(trc) >= 2 else 0

                        def _(te, sg=sg, fk=fk, slot=slot, guard=guard,
                              n=he_all):
                            te.wait_ge(vs.h, n)
                            if guard:
                                te.wait_ge(vs.h, guard)
                            ts.inc(te.transpose(
                                psTr[slot][:, 0:128],
                                he[:, sg, fk * 128:(fk + 1) * 128], ident[:]))
                        blk.tensor(_)
                        tr_n = ts.n

                        def _(ve, sg=sg, fk=fk, slot=slot, n=tr_n):
                            ve.wait_ge(ts.h, n)
                            vs.inc(ve.tensor_copy(
                                hT[:, fk, sg * 128:(sg + 1) * 128],
                                psTr[slot][:, 0:128]))
                        blk.vector(_)
                        trc.append(vs.n)
                ht_ready = trc[-1]
                for sg in range(2):
                    for j in range(4):
                        he_guard[(sg, j)] = ts.n

                dn_rd = 0
                for nh in range(2):
                    q = e * 2 + nh
                    for sg in range(2):
                        def _(te, q=q, nh=nh, sg=sg, n=ht_ready, dr=dn_rd,
                              yg=ysb_guard):
                            te.wait_ge(vs.h, n)
                            te.wait_ge(_dw, 256 * (q + 1))
                            if dr:
                                te.wait_ge(vs.h, dr)
                            if yg and nh == 0 and sg == 0:
                                te.wait_ge(gp.h, yg)
                            for fk in range(16):
                                mm = te.matmul(psDn[:, 0:512],
                                               hT[:, fk, sg * 128:(sg + 1) * 128],
                                               Dbuf[:, fk, :],
                                               start=(fk == 0), stop=(fk == 15))
                            mm.then_inc(_df if sg == 1 else _ts, 1)
                        blk.tensor(_)
                        if sg == 0:
                            ts.n += 1
                            dn_wait = ("ts", ts.n)
                        else:
                            df_cnt[0] += 1
                            dn_wait = ("df", df_cnt[0])

                        def _(ve, nh=nh, sg=sg, w=dn_wait):
                            ve.wait_ge(ts.h if w[0] == "ts" else _df, w[1])
                            vs.inc(ve.tensor_copy(
                                ysb[:, sg, nh * 512:(nh + 1) * 512],
                                psDn[:, 0:512]))
                        blk.vector(_)
                        dn_rd = vs.n
                y_done = vs.n

                def _(g, e=e, n=y_done):
                    g.wait_ge(vs.h, n)
                    for sg in range(2):
                        gp.inc(g.dma_start(
                            out=y_hbm[e * CAP + sg * 128:
                                      e * CAP + (sg + 1) * 128, :],
                            in_=ysb[:, sg]), 16)
                blk.gpsimd(_)
                ysb_guard = gp.n
            moe_ts = ts.n
            moe_vs = vs.n
            moe_gp = gp.n
        es1.close()

        es2 = ExitStack()
        _E = es2.enter_context
        W2T = _E(nc.sbuf_tensor("W2T_sb", [128, 8, DM], F32))
        Vsb = _E(nc.sbuf_tensor("Vsb_sb", [128, 8, DM], F32))
        es2b = ExitStack()
        _E = es2b.enter_context
        cTs = _E(nc.sbuf_tensor("cTs_sb", [64, H, S], F32))
        wo64 = _E(nc.sbuf_tensor("wo64_sb", [64, H, DM], F32))
        wo128 = _E(nc.sbuf_tensor("wo128_sb", [128, 8, DM], F32))
        skipc = _E(nc.sbuf_tensor("skipc_sb", [128, 8], F32))
        vtmp = _E(nc.sbuf_tensor("vtmp_sb", [64, 512], F32))
        psV = _E(nc.psum_tensor("psV", [128, 512], F32))
        if True:
            def _(sy, m=moe_vs, mt=moe_ts, mg=moe_gp):
                sy.wait_ge(vs.h, m)
                sy.wait_ge(ts.h, mt)
                sy.wait_ge(gp.h, mg)
                ld.inc(sy.dma_start(out=cTs[:, :, :],
                                    in_=cT_h.rearrange("h d s -> d h s")), 16)
                ld.inc(sy.dma_start(out=wo64[:, :, :],
                                    in_=woT_h.rearrange("(h p) o -> p h o", p=64)),
                       16)
                ld.inc(sy.dma_start(out=wo128[:, :, :],
                                    in_=woT_h.rearrange("(c p) o -> p c o", p=128)),
                       16)
                ld.inc(sy.dma_start(out=skipc[:, :], in_=skip_h[:, :]), 16)
            blk.sync(_)
            vw_ld = ld.n

            def _(ve, n=vw_ld):
                ve.wait_ge(ld.h, n)
                for c in range(8):
                    ii = ve.tensor_scalar(W2T[:, c], wo128[:, c], skipc[:, c:c + 1],
                                          None, op0=ALU.mult)
                vs.inc(ii)
            blk.vector(_)
            w2t_done = vs.n

            vcopy = 0
            for h in range(H if L3_PH >= 3 else 0):
                for no in range(2):
                    def _(te, h=h, no=no, n=vw_ld, m=vcopy, g0=gp.n):
                        te.wait_ge(ld.h, n)
                        if m:
                            te.wait_ge(vs.h, m)
                            te.wait_ge(gp.h, g0)
                        ts.inc(te.matmul(psV[0:64, 0:512], cTs[:, h, :],
                                         wo64[:, h, no * 512:(no + 1) * 512],
                                         start=True, stop=True))
                    blk.tensor(_)
                    v_mm = ts.n

                    def _(ve, n=v_mm):
                        ve.wait_ge(ts.h, n)
                        vs.inc(ve.tensor_copy(vtmp[:, :], psV[0:64, 0:512]))
                    blk.vector(_)
                    vcopy = vs.n

                    def _(g, h=h, no=no, n=vcopy):
                        g.wait_ge(vs.h, n)
                        gp.inc(g.dma_start(
                            out=v_hbm[h * 64:(h + 1) * 64,
                                      no * 512:(no + 1) * 512],
                            in_=vtmp[:, :]), 16)
                    blk.gpsimd(_)
            v_stores = gp.n

            def _(sy, n=v_stores):
                sy.wait_ge(gp.h, n)
                ld.inc(sy.dma_start(out=Vsb[:, :, :],
                                    in_=v_hbm.ap().rearrange("(c p) o -> p c o",
                                                             p=128)), 16)
            blk.sync(_)
            v_ready = ld.n
            es2b.close()

            es3 = ExitStack()
            _E = es3.enter_context
            outacc = _E(nc.sbuf_tensor("outacc_sb", [128, NTILE, DM], F32))
            es3b = ExitStack()
            _E = es3b.enter_context
            wgb = _E(nc.sbuf_tensor("wgb_sb", [128, 8, 512], F32))
            wub = _E(nc.sbuf_tensor("wub_sb", [128, 8, 512], F32))
            wdb = _E(nc.sbuf_tensor("wdb_sb", [128, 4, DM], F32))
            xnTt = _E(nc.sbuf_tensor("xnTt_sb", [128, 8, 128], F32))
            hsh = _E(nc.sbuf_tensor("hsh_sb", [128, 512], F32))
            hshT = _E(nc.sbuf_tensor("hshT_sb", [128, 4, 128], F32))
            silu2 = _E(nc.sbuf_tensor("silu2_sb", [128, 512], F32))
            psG2 = _E(nc.psum_tensor("psG2", [128, 512], F32))
            psU2b = _E(nc.psum_tensor("psU2", [128, 512], F32))
            psD2 = _E(nc.psum_tensor("psD2", [128, 512], F32))
            psTa = _E(nc.psum_tensor("psTa", [128, 512], F32))
            psTb = _E(nc.psum_tensor("psTb", [128, 512], F32))
            if True:
                psT2 = [psTa, psTb]

                def _(ve, m=moe_vs, mt=moe_ts, mg=moe_gp):
                    ve.wait_ge(ts.h, mt)
                    ve.wait_ge(gp.h, mg)
                    vs.inc(ve.memset(outacc[:, :, :], 0.0))
                blk.vector(_)
                oz = vs.n

                wld = 0
                trc2 = []
                silu2_rd = 0
                hsh_rd = 0
                hsh_rd0 = [0]
                acc_wr = oz
                for ng in range(4 if L3_PH >= 4 else 0):
                    def _(sy, ng=ng, w=wld, mt=moe_ts, mg=moe_gp, mv=moe_vs):
                        if ng == 0 and not w:
                            sy.wait_ge(ts.h, mt)
                            sy.wait_ge(gp.h, mg)
                            sy.wait_ge(vs.h, mv)
                        if w:
                            sy.wait_ge(ts.h, w)
                        for dk in range(8):
                            ld.inc(sy.dma_start(
                                out=wgb[:, dk, :],
                                in_=wgT_h[dk * 128:(dk + 1) * 128,
                                          ng * 512:(ng + 1) * 512]), 16)
                        for dk in range(8):
                            ld.inc(sy.dma_start(
                                out=wub[:, dk, :],
                                in_=wuT_h[dk * 128:(dk + 1) * 128,
                                          ng * 512:(ng + 1) * 512]), 16)
                        for fk in range(4):
                            ld.inc(sy.dma_start(
                                out=wdb[:, fk, :],
                                in_=wdT_h[ng * 512 + fk * 128:
                                          ng * 512 + (fk + 1) * 128, :]), 16)
                    blk.sync(_)
                    ng_ld = ld.n

                    for i in range(NTILE):
                        def _(sy, ng=ng, i=i, w=acc_wr):
                            sy.wait_ge(vs.h, w)
                            ld.inc(sy.dma_start(
                                out=xnTt[:, :, :],
                                in_=xnT_h[:, :, i * 128:(i + 1) * 128]
                                .rearrange("c p t -> p c t")), 16)
                        blk.sync(_)
                        t_ld = ld.n

                        def _(te, ng=ng, i=i, n=t_ld, m=ng_ld, sr=silu2_rd,
                              hr=hsh_rd):
                            te.wait_ge(ld.h, max(n, m))
                            if sr:
                                te.wait_ge(ss.h, sr)
                            if hr:
                                te.wait_ge(vs.h, hr)
                            for dk in range(8):
                                mm = te.matmul(psG2[:, 0:512], xnTt[:, dk],
                                               wgb[:, dk],
                                               start=(dk == 0), stop=(dk == 7))
                            ts.inc(mm)
                            for dk in range(8):
                                mm = te.matmul(psU2b[:, 0:512], xnTt[:, dk],
                                               wub[:, dk],
                                               start=(dk == 0), stop=(dk == 7))
                            ts.inc(mm)
                        blk.tensor(_)
                        gu2 = ts.n - 1

                        def _(sc, n=gu2, hf=hsh_rd0[0]):
                            sc.wait_ge(ts.h, n)
                            if hf:
                                sc.wait_ge(vs.h, hf)
                            ss.inc(sc.activation(silu2[:, :], psG2[:, 0:512],
                                                 AF.Silu))
                        blk.scalar(_)
                        silu2_rd = ss.n

                        def _(ve, n=silu2_rd, m=ts.n):
                            ve.wait_ge(ss.h, n)
                            ve.wait_ge(ts.h, m)
                            vs.inc(ve.tensor_tensor(hsh[:, :], silu2[:, :],
                                                    psU2b[:, 0:512], op=ALU.mult))
                        blk.vector(_)
                        hsh_done = vs.n
                        hsh_rd0[0] = vs.n

                        for fk in range(4):
                            slot = len(trc2) % 2
                            guard = trc2[-2] if len(trc2) >= 2 else 0

                            def _(te, fk=fk, slot=slot, guard=guard, n=hsh_done):
                                te.wait_ge(vs.h, n)
                                if guard:
                                    te.wait_ge(vs.h, guard)
                                ts.inc(te.transpose(
                                    psT2[slot][:, 0:128],
                                    hsh[:, fk * 128:(fk + 1) * 128], ident[:]))
                            blk.tensor(_)
                            tr2 = ts.n

                            def _(ve, fk=fk, slot=slot, n=tr2):
                                ve.wait_ge(ts.h, n)
                                vs.inc(ve.tensor_copy(hshT[:, fk],
                                                      psT2[slot][:, 0:128]))
                            blk.vector(_)
                            trc2.append(vs.n)
                        hsh_rd = trc2[-1]

                        for nh in range(2):
                            def _(te, nh=nh, n=hsh_rd, w=acc_wr):
                                te.wait_ge(vs.h, max(n, w))
                                for fk in range(4):
                                    mm = te.matmul(psD2[:, 0:512], hshT[:, fk],
                                                   wdb[:, fk,
                                                       nh * 512:(nh + 1) * 512],
                                                   start=(fk == 0), stop=(fk == 3))
                                ts.inc(mm)
                            blk.tensor(_)
                            d2 = ts.n

                            def _(ve, i=i, nh=nh, n=d2):
                                ve.wait_ge(ts.h, n)
                                vs.inc(ve.tensor_tensor(
                                    outacc[:, i, nh * 512:(nh + 1) * 512],
                                    outacc[:, i, nh * 512:(nh + 1) * 512],
                                    psD2[:, 0:512], op=ALU.add))
                            blk.vector(_)
                            acc_wr = vs.n
                    wld = ts.n
                shared_done = vs.n
                es3b.close()

                es4 = ExitStack()
                _E = es4.enter_context
                stTt = _E(nc.sbuf_tensor("stTt_sb", [128, 8, 128], F32))
                x2Tt = _E(nc.sbuf_tensor("x2Tt_sb", [128, 8, 128], F32))
                xnt2 = _E(nc.sbuf_tensor("xnt2_sb", [128, DM], F32))
                xt2 = _E(nc.sbuf_tensor("xt2_sb", [128, DM], F32))
                yg = _E(nc.sbuf_tensor("yg_sb", [128, DM], F32))
                gsb = _E(nc.sbuf_tensor("gsb_sb", [128, DM], F32))
                ssb = _E(nc.sbuf_tensor("ssb_sb", [128, DM], F32))
                osb = _E(nc.sbuf_tensor("osb_sb", [128, DM], F32))
                psS0 = _E(nc.psum_tensor("psS0", [128, 512], F32))
                psS1 = _E(nc.psum_tensor("psS1", [128, 512], F32))
                if True:
                    psS = [psS0, psS1]
                    ep_prev = 0
                    ep_store = 0
                    if L3_PH < 5:
                        def _(g):
                            for i in range(NTILE):
                                gp.inc(g.dma_start(
                                    out=out_h[i * 128:(i + 1) * 128, :],
                                    in_=yg[:, :]), 16)
                            g.wait_ge(gp.h, gp.n)
                        blk.gpsimd(_)
                    for i in range(NTILE if L3_PH >= 5 else 0):
                        def _(sy, i=i, n=v_ready, w=ep_prev, sd=shared_done,
                              st_=ts.n):
                            sy.wait_ge(ld.h, n)
                            if i == 0:
                                sy.wait_ge(vs.h, sd)
                                sy.wait_ge(ts.h, st_)
                            if w:
                                sy.wait_ge(vs.h, w)
                            ld.inc(sy.dma_start(
                                out=stTt[:, :, :],
                                in_=stT_h[:, :, i * 128:(i + 1) * 128]
                                .rearrange("c p t -> p c t")), 16)
                            ld.inc(sy.dma_start(
                                out=x2Tt[:, :, :],
                                in_=x2T_h[:, :, i * 128:(i + 1) * 128]
                                .rearrange("c p t -> p c t")), 16)
                            ld.inc(sy.dma_start(
                                out=xnt2[:, :],
                                in_=xn_h[i * 128:(i + 1) * 128, 1:DM + 1]), 16)
                            ld.inc(sy.dma_start(
                                out=xt2[:, :],
                                in_=x_h[i * 128:(i + 1) * 128, :]), 16)
                        blk.sync(_)
                        ep_ld = ld.n

                        def _(g, i=i, n=shared_done):
                            g.wait_ge(vs.h, n)
                            gp.inc(g.indirect_dma_start(
                                out=yg[:, :], out_offset=None,
                                in_=y_hbm[:, :],
                                in_offset=bass.IndirectOffsetOnAxis(
                                    dst_i[:, i:i + 1], 0)), 16)
                        blk.gpsimd(_)
                        yg_ld = gp.n

                        def _(te, n=ep_ld, w=ep_prev, m=w2t_done):
                            te.wait_ge(ld.h, n)
                            te.wait_ge(vs.h, m)
                            if w:
                                te.wait_ge(vs.h, w)
                            for no in range(2):
                                for c in range(8):
                                    te.matmul(psS[no][:, 0:512], stTt[:, c],
                                              Vsb[:, c, no * 512:(no + 1) * 512],
                                              start=(c == 0), stop=False)
                                for c in range(8):
                                    te.matmul(psS[no][:, 0:512], x2Tt[:, c],
                                              W2T[:, c, no * 512:(no + 1) * 512],
                                              start=False, stop=False)
                                mm = te.matmul(psS[no][:, 0:512], onesc[0:1, :],
                                               bout[:, no * 512:(no + 1) * 512],
                                               start=False, stop=True)
                            ts.inc(mm)
                        blk.tensor(_)
                        s_mm = ts.n

                        def _(ve, i=i, n=s_mm, m=yg_ld):
                            ve.wait_ge(ts.h, n)
                            ve.wait_ge(gp.h, m)
                            for no in range(2):
                                ve.tensor_tensor(
                                    ssb[:, no * 512:(no + 1) * 512],
                                    psS[no][:, 0:512],
                                    xnt2[:, no * 512:(no + 1) * 512], op=ALU.add)
                            vs.inc(ve.tensor_tensor(gsb[:, :], outacc[:, i, :],
                                                    yg[:, :], op=ALU.add))
                        blk.vector(_)
                        gpre_done = vs.n

                        def _(sc, n=gpre_done):
                            sc.wait_ge(vs.h, n)
                            ss.inc(sc.activation(gsb[:, :], gsb[:, :], AF.Silu))
                        blk.scalar(_)
                        g_done = ss.n

                        def _(ve, n=g_done, w=ep_store if i > 0 else 0):
                            ve.wait_ge(ss.h, n)
                            if w:
                                ve.wait_ge(gp.h, w)
                            ve.drain()
                            ve.tensor_tensor(gsb[:, :], ssb[:, :], gsb[:, :],
                                             op=ALU.mult)
                            ve.drain()
                            vs.inc(ve.tensor_tensor(osb[:, :], xt2[:, :],
                                                    gsb[:, :], op=ALU.add))
                        blk.vector(_)
                        o_done = vs.n

                        def _(g, i=i, n=o_done):
                            g.wait_ge(vs.h, n)
                            gp.inc(g.dma_start(
                                out=out_h[i * 128:(i + 1) * 128, :],
                                in_=osb[:, :]), 16)
                        blk.gpsimd(_)
                        ep_prev = o_done
                        ep_store = gp.n

                    def _(g, n=gp.n):
                        g.wait_ge(gp.h, n)
                    blk.gpsimd(_)
                es4.close()
            es3.close()
        es2.close()
    es0.close()
    return nc


_NC_CACHE = {}


def _get(name, builder):
    if name not in _NC_CACHE:
        _NC_CACHE[name] = builder()
    return _NC_CACHE[name]


def kernel(L0_attn_tokens, L0_attn_norm_w, L0_attn_a, L0_attn_b, L0_attn_c,
           L0_attn_skip, L0_attn_w_out, L0_attn_b_out, L0_ffn_w_router,
           L0_ffn_gate_up_proj, L0_ffn_down_proj, L0_ffn_w_gate, L0_ffn_w_up,
           L0_ffn_w_down):
    f32 = np.float32
    x = np.ascontiguousarray(np.asarray(L0_attn_tokens, f32).reshape(NTOK, DM))
    norm_w = np.asarray(L0_attn_norm_w, f32)
    a_in = np.asarray(L0_attn_a, f32)
    b_in = np.asarray(L0_attn_b, f32)
    c_in = np.asarray(L0_attn_c, f32)
    skip = np.asarray(L0_attn_skip, f32)
    w_out = np.asarray(L0_attn_w_out, f32)
    b_out = np.asarray(L0_attn_b_out, f32)
    w_router = np.asarray(L0_ffn_w_router, f32)
    gup = np.ascontiguousarray(np.asarray(L0_ffn_gate_up_proj, f32))
    dwn = np.ascontiguousarray(np.asarray(L0_ffn_down_proj, f32))
    w_gate = np.asarray(L0_ffn_w_gate, f32)
    w_up = np.asarray(L0_ffn_w_up, f32)
    w_down = np.asarray(L0_ffn_w_down, f32)

    ident = np.eye(128, dtype=f32)
    ltri = np.tril(np.ones((128, 128), f32), -1)
    iota = np.broadcast_to(np.arange(E, dtype=f32), (128, E)).copy()
    iota99 = iota - 99.0
    onesc = np.ones((128, 128), f32)
    nwrep = np.broadcast_to(norm_w, (128, DM)).copy()
    wrt = np.ascontiguousarray(w_router.T)
    tokid = (np.arange(TPC, dtype=np.int32).reshape(NTILE, 128).T).copy()
    fillc = np.full((128, 2112), TPC, dtype=np.int32)
    wgT = np.ascontiguousarray(w_gate.T)
    wuT = np.ascontiguousarray(w_up.T)
    wdT = np.ascontiguousarray(w_down.T)
    cT = np.ascontiguousarray(np.transpose(c_in, (0, 2, 1)))
    woT = np.ascontiguousarray(w_out.T)
    skipc = np.ascontiguousarray(skip.reshape(DM).reshape(8, 128).T)
    boutr = np.ascontiguousarray(b_out.reshape(1, DM))

    nc1 = _get("l1", build_l1)
    in1 = []
    for k in range(NCORE):
        in1.append({
            "x": x[k * TPC:(k + 1) * TPC],
            "nwrep": nwrep, "wrt": wrt, "ident": ident, "ltri": ltri,
            "iota": iota, "iota99": iota99, "onesc": onesc,
        })
    r1 = run_bass_kernel_spmd(nc1, in1, core_ids=list(range(NCORE)))
    xn_all = [r1.results[k]["xn"] for k in range(NCORE)]
    xnT_all = [r1.results[k]["xnT"] for k in range(NCORE)]
    x2T_all = [r1.results[k]["x2T"] for k in range(NCORE)]
    dest_all = [r1.results[k]["dest"].T.reshape(TPC) for k in range(NCORE)]
    score_all = [r1.results[k]["score"].T.reshape(TPC) for k in range(NCORE)]

    in2 = []
    for k in range(NCORE):
        ch, po = k // 1, 0
        x2w = np.empty((NW, 128, W, B), f32)
        for w in range(NW):
            t0 = w * W
            for b in range(B):
                srcc = 2 * b + (1 if t0 >= T // 2 else 0)
                tloc = t0 - (T // 2 if t0 >= T // 2 else 0)
                x2w[w, :, :, b] = x2T_all[srcc][k, :, tloc:tloc + W].transpose(0, 1)
        x2w = np.ascontiguousarray(x2w.reshape(NW, 128, 4 * W))
        in2.append({
            "a2": a_in[2 * k:2 * k + 2],
            "b2": b_in[2 * k:2 * k + 2],
            "x2w": x2w,
        })
    nc2 = _get("l2", build_l2)
    r2 = run_bass_kernel_spmd(nc2, in2, core_ids=list(range(NCORE)))
    states_all = [r2.results[k]["states"] for k in range(NCORE)]

    in3 = []
    for k in range(NCORE):
        bk = k // 2
        th = (k % 2) * (T // 2)
        stT = np.empty((8, 128, TPC), f32)
        for m in range(8):
            sa = states_all[m]
            for w in range(T // 2 // W):
                wg = (th // W) + w
                stT[m, :, w * W:(w + 1) * W] = \
                    sa[wg].reshape(128, W, B)[:, :, bk]
        sc_p = np.concatenate([score_all[k], np.zeros(1, f32)])[:, None]
        xn_p = np.concatenate([xn_all[k], np.zeros((1, DM), f32)], axis=0)
        xn_p = np.ascontiguousarray(np.concatenate(
            [sc_p, xn_p, np.zeros((TPC + 1, 1088 - DM - 1), f32)], axis=1))
        dest_i = np.ascontiguousarray(
            dest_all[k].reshape(NTILE, 128).T.astype(np.int32))
        in3.append({
            "xn": xn_p, "xnT": xnT_all[k], "x2T": x2T_all[k],
            "x": x[k * TPC:(k + 1) * TPC], "stT": stT,
            "dest": dest_i, "tokid": tokid,
            "gup": gup, "dwn": dwn, "wgT": wgT, "wuT": wuT, "wdT": wdT,
            "cT": cT, "woT": woT, "skipc": skipc, "bout": boutr,
            "ident": ident, "onesc": onesc, "fillc": fillc,
        })
    nc3 = _get("l3", build_l3)
    r3 = run_bass_kernel_spmd(nc3, in3, core_ids=list(range(NCORE)))
    out = np.concatenate([r3.results[k]["out"] for k in range(NCORE)], axis=0)
    return out.reshape(B, T, DM)


# version 16
# speedup vs baseline: 1.0199x; 1.0199x over previous
import numpy as np
from contextlib import ExitStack
import concourse.bass as bass
import concourse.mybir as mybir
from concourse.bass_utils import run_bass_kernel_spmd

F32 = mybir.dt.float32
I32 = mybir.dt.int32
AF = mybir.ActivationFunctionType
ALU = mybir.AluOpType
AX = mybir.AxisListType.X

NCORE = 8
B, T, DM = 4, 4096, 1024
NTOK = B * T
TPC = NTOK // NCORE
NTILE = TPC // 128
H, D, S = 16, 64, 64
E, FF = 16, 2048
CAP = 256
NSLOT = E * CAP
TRASH = NSLOT
W = 512
NW = T // W
LEPS = 1e-5
REPS = 1e-4
L3_PH = 5


class Sem:
    def __init__(self, h):
        self.h = h
        self.n = 0

    def inc(self, inst, k=1):
        if k:
            inst.then_inc(self.h, k)
        self.n += k
        return self.n


def build_l1():
    nc = bass.Bass("TRN2", target_bir_lowering=False, debug=False, detect_race_conditions=False)
    x_h = nc.dram_tensor("x", [TPC, DM], F32, kind="ExternalInput").ap()
    nwrep_h = nc.dram_tensor("nwrep", [128, DM], F32, kind="ExternalInput").ap()
    wrt_h = nc.dram_tensor("wrt", [DM, E], F32, kind="ExternalInput").ap()
    ident_h = nc.dram_tensor("ident", [128, 128], F32, kind="ExternalInput").ap()
    ltri_h = nc.dram_tensor("ltri", [128, 128], F32, kind="ExternalInput").ap()
    iota_h = nc.dram_tensor("iota", [128, E], F32, kind="ExternalInput").ap()
    iota99_h = nc.dram_tensor("iota99", [128, E], F32, kind="ExternalInput").ap()
    ones_h = nc.dram_tensor("onesc", [128, 128], F32, kind="ExternalInput").ap()

    xn_h = nc.dram_tensor("xn", [TPC, DM], F32, kind="ExternalOutput").ap()
    xnT_h = nc.dram_tensor("xnT", [8, 128, TPC], F32, kind="ExternalOutput").ap()
    x2T_h = nc.dram_tensor("x2T", [8, 128, TPC], F32, kind="ExternalOutput").ap()
    dest_h = nc.dram_tensor("dest", [128, NTILE], I32, kind="ExternalOutput").ap()
    score_h = nc.dram_tensor("score", [128, NTILE], F32, kind="ExternalOutput").ap()

    es = ExitStack()
    _E = es.enter_context
    nwrep = _E(nc.sbuf_tensor("nwrep_sb", [128, DM], F32))
    wrt = _E(nc.sbuf_tensor("wrt_sb", [128, 8, E], F32))
    ident = _E(nc.sbuf_tensor("ident_sb", [128, 128], F32))
    ltri = _E(nc.sbuf_tensor("ltri_sb", [128, 128], F32))
    iota = _E(nc.sbuf_tensor("iota_sb", [128, E], F32))
    iota99 = _E(nc.sbuf_tensor("iota99_sb", [128, E], F32))
    onesc = _E(nc.sbuf_tensor("onesc_sb", [128, 128], F32))
    xt = _E(nc.sbuf_tensor("xt_sb", [128, 2, DM], F32))
    xn_t = _E(nc.sbuf_tensor("xn_t_sb", [128, DM], F32))
    x2_t = _E(nc.sbuf_tensor("x2_t_sb", [128, DM], F32))
    sq = _E(nc.sbuf_tensor("sq_sb", [128, DM], F32))
    xnT_t = _E(nc.sbuf_tensor("xnT_t_sb", [128, 8, 128], F32))
    x2T_t = _E(nc.sbuf_tensor("x2T_t_sb", [128, 8, 128], F32))
    cols = _E(nc.sbuf_tensor("cols_sb", [128, 16], F32))
    sig = _E(nc.sbuf_tensor("sig_sb", [128, E], F32))
    mask = _E(nc.sbuf_tensor("mask_sb", [128, E], F32))
    tmpe = _E(nc.sbuf_tensor("tmpe_sb", [128, E], F32))
    prefix = _E(nc.sbuf_tensor("prefix_sb", [1, E], F32))
    dst_i = _E(nc.sbuf_tensor("dst_i_sb", [128, NTILE], I32))
    sc_s = _E(nc.sbuf_tensor("sc_s_sb", [128, NTILE], F32))
    epsc = _E(nc.sbuf_tensor("epsc_sb", [128, 2], F32))
    psT0 = _E(nc.psum_tensor("psT0", [128, 512], F32))
    psT1 = _E(nc.psum_tensor("psT1", [128, 512], F32))
    psL = _E(nc.psum_tensor("psL", [128, 512], F32))
    psR = _E(nc.psum_tensor("psR", [128, 512], F32))
    psC = _E(nc.psum_tensor("psC", [128, 512], F32))
    _ld = _E(nc.semaphore("ld"))
    _vs = _E(nc.semaphore("vs"))
    _ss = _E(nc.semaphore("ss"))
    _ts = _E(nc.semaphore("ts"))
    _st = _E(nc.semaphore("st"))
    blk = _E(nc.Block())
    if True:
        ld, vs, ss, ts, st = Sem(_ld), Sem(_vs), Sem(_ss), Sem(_ts), Sem(_st)
        psT = [psT0, psT1]

        def _(sy):
            for dst, src in [(nwrep, nwrep_h), (ident, ident_h), (ltri, ltri_h),
                             (iota, iota_h), (iota99, iota99_h), (onesc, ones_h)]:
                ld.inc(sy.dma_start(out=dst[:], in_=src[:, :]), 16)
            ld.inc(sy.dma_start(out=wrt[:, :, :],
                                in_=wrt_h.rearrange("(c p) e -> p c e", p=128)), 16)
        blk.sync(_)

        def _(ve):
            ve.memset(prefix[:], 0.0)
            ve.memset(epsc[:, 0:1], LEPS)
            vs.inc(ve.memset(epsc[:, 1:2], REPS))
        blk.vector(_)

        last_ts = 0
        last_ss = 0
        last_st = 0
        xt_rd = {}
        trc = []

        for i in range(NTILE):
            sl = i % 2

            def _(sy, i=i, sl=sl, w=xt_rd.get(i - 2, 0)):
                if w:
                    sy.wait_ge(vs.h, w)
                ld.inc(sy.dma_start(out=xt[:, sl],
                                    in_=x_h[i * 128:(i + 1) * 128, :]), 16)
            blk.sync(_)
            x_ld = ld.n

            def _(ve, sl=sl, x_ld=x_ld, a=last_ts, b=last_ss, c=last_st):
                ve.wait_ge(ld.h, x_ld)
                if a:
                    ve.wait_ge(ts.h, a)
                if b:
                    ve.wait_ge(ss.h, b)
                if c:
                    ve.wait_ge(st.h, c)
                ve.reduce_sum(cols[:, 0:1], xt[:, sl], axis=AX)
                ve.drain()
                ve.tensor_scalar(cols[:, 1:2], cols[:, 0:1], 1.0 / DM, None,
                                 op0=ALU.mult)
                ve.drain()
                vs.inc(ve.tensor_scalar(xn_t[:, :], xt[:, sl], cols[:, 1:2], None,
                                        op0=ALU.subtract))
            blk.vector(_)
            xt_rd[i] = vs.n
            xm_done = vs.n

            def _(sc, n=xm_done):
                sc.wait_ge(vs.h, n)
                ss.inc(sc.activation(sq[:, :], xn_t[:, :], AF.Square))
            blk.scalar(_)
            sq1 = ss.n

            def _(ve, n=sq1):
                ve.wait_ge(ss.h, n)
                ve.reduce_sum(cols[:, 2:3], sq[:, :], axis=AX)
                ve.drain()
                vs.inc(ve.tensor_scalar(cols[:, 3:4], cols[:, 2:3], 1.0 / DM, None,
                                        op0=ALU.mult))
            blk.vector(_)
            var_done = vs.n

            def _(sc, n=var_done):
                sc.wait_ge(vs.h, n)
                ss.inc(sc.activation(cols[:, 4:5], cols[:, 3:4], AF.Sqrt, bias=epsc[:, 0:1]))
            blk.scalar(_)
            std_done = ss.n

            def _(ve, n=std_done):
                ve.wait_ge(ss.h, n)
                ve.reciprocal(cols[:, 5:6], cols[:, 4:5])
                ve.drain()
                vs.inc(ve.tensor_scalar(xn_t[:, :], xn_t[:, :], cols[:, 5:6], None,
                                        op0=ALU.mult))
            blk.vector(_)
            xn_done = vs.n

            def _(sy, i=i, n=xn_done):
                sy.wait_ge(vs.h, n)
                st.inc(sy.dma_start(out=xn_h[i * 128:(i + 1) * 128, :],
                                    in_=xn_t[:, :]), 16)
            blk.sync(_)

            def _(sc, n=xn_done):
                sc.wait_ge(vs.h, n)
                ss.inc(sc.activation(sq[:, :], xn_t[:, :], AF.Square))
            blk.scalar(_)
            sq2 = ss.n

            def _(ve, n=sq2):
                ve.wait_ge(ss.h, n)
                ve.reduce_sum(cols[:, 6:7], sq[:, :], axis=AX)
                ve.drain()
                vs.inc(ve.tensor_scalar(cols[:, 7:8], cols[:, 6:7], 1.0 / DM, None,
                                        op0=ALU.mult))
            blk.vector(_)
            ms_done = vs.n

            def _(sc, n=ms_done):
                sc.wait_ge(vs.h, n)
                ss.inc(sc.activation(cols[:, 8:9], cols[:, 7:8], AF.Sqrt, bias=epsc[:, 1:2]))
            blk.scalar(_)
            rstd_done = ss.n

            def _(ve, n=rstd_done):
                ve.wait_ge(ss.h, n)
                ve.reciprocal(cols[:, 9:10], cols[:, 8:9])
                ve.drain()
                vs.inc(ve.scalar_tensor_tensor(x2_t[:, :], xn_t[:, :], cols[:, 9:10],
                                               nwrep[:], op0=ALU.mult, op1=ALU.mult))
            blk.vector(_)
            x2_done = vs.n

            for c in range(16):
                cc = c % 8
                slot = len(trc) % 2
                guard = trc[-2] if len(trc) >= 2 else 0

                def _(te, c=c, cc=cc, slot=slot, guard=guard,
                      xn_n=xn_done, x2_n=x2_done):
                    te.wait_ge(vs.h, x2_n if c >= 8 else xn_n)
                    if guard:
                        te.wait_ge(vs.h, guard)
                    src = xn_t if c < 8 else x2_t
                    ts.inc(te.transpose(psT[slot][:, 0:128],
                                        src[:, cc * 128:(cc + 1) * 128], ident[:]))
                blk.tensor(_)
                tr_n = ts.n

                def _(ve, c=c, cc=cc, slot=slot, n=tr_n):
                    ve.wait_ge(ts.h, n)
                    dst = xnT_t[:, cc] if c < 8 else x2T_t[:, cc]
                    vs.inc(ve.tensor_copy(dst, psT[slot][:, 0:128]))
                blk.vector(_)
                trc.append(vs.n)
            xnT_done = trc[-9]
            x2T_done = trc[-1]

            def _(sy, i=i, a=xnT_done, b=x2T_done):
                sy.wait_ge(vs.h, a)
                st.inc(sy.dma_start(
                    out=xnT_h[:, :, i * 128:(i + 1) * 128]
                    .rearrange("c p t -> p c t"), in_=xnT_t[:, :, :]), 16)
                sy.wait_ge(vs.h, b)
                st.inc(sy.dma_start(
                    out=x2T_h[:, :, i * 128:(i + 1) * 128]
                    .rearrange("c p t -> p c t"), in_=x2T_t[:, :, :]), 16)
            blk.sync(_)

            def _(te, n=xnT_done):
                te.wait_ge(vs.h, n)
                for c in range(8):
                    mm = te.matmul(psL[:, 0:E], xnT_t[:, c], wrt[:, c, :],
                                   start=(c == 0), stop=(c == 7))
                ts.inc(mm)
            blk.tensor(_)
            logits_done = ts.n

            def _(sc, n=logits_done):
                sc.wait_ge(ts.h, n)
                ss.inc(sc.activation(sig[:, :], psL[:, 0:E], AF.Sigmoid))
            blk.scalar(_)
            sig_done = ss.n

            def _(ve, i=i, n=logits_done, m=sig_done):
                ve.wait_ge(ts.h, n)
                ve.wait_ge(ss.h, m)
                ve.reduce_max(cols[:, 10:11], psL[:, 0:E], axis=AX)
                ve.drain()
                ve.tensor_scalar(mask[:, :], psL[:, 0:E], cols[:, 10:11], None,
                                 op0=ALU.is_ge)
                ve.drain()
                ve.tensor_tensor(tmpe[:, :], mask[:, :], iota99[:, :], op=ALU.mult)
                ve.drain()
                ve.tensor_scalar(tmpe[:, :], tmpe[:, :], 99.0, None, op0=ALU.add)
                ve.drain()
                ve.tensor_reduce(cols[:, 11:12], tmpe[:, :], axis=AX, op=ALU.min)
                ve.drain()
                ve.tensor_scalar(mask[:, :], iota[:, :], cols[:, 11:12], None,
                                 op0=ALU.is_equal)
                ve.drain()
                ve.tensor_tensor(tmpe[:, :], sig[:, :], mask[:, :], op=ALU.mult)
                ve.drain()
                vs.inc(ve.reduce_sum(sc_s[:, i:i + 1], tmpe[:, :], axis=AX))
            blk.vector(_)
            mask_done = vs.n

            def _(te, n=mask_done):
                te.wait_ge(vs.h, n)
                te.matmul(psR[:, 0:E], ltri[:], mask[:, :], start=True, stop=False)
                te.matmul(psR[:, 0:E], onesc[0:1, :], prefix[:, :],
                          start=False, stop=True)
                ts.inc(te.matmul(psC[0:1, 0:E], onesc[:, 0:1], mask[:, :],
                                 start=True, stop=True))
            blk.tensor(_)
            rank_done = ts.n

            def _(ve, i=i, n=rank_done):
                ve.wait_ge(ts.h, n)
                ve.tensor_tensor(tmpe[:, :], psR[:, 0:E], mask[:, :], op=ALU.mult)
                ve.drain()
                ve.reduce_sum(cols[:, 12:13], tmpe[:, :], axis=AX)
                ve.tensor_tensor(prefix[:, :], prefix[:, :], psC[0:1, 0:E],
                                 op=ALU.add)
                ve.tensor_scalar(cols[:, 13:14], cols[:, 11:12], float(CAP), None,
                                 op0=ALU.mult)
                ve.drain()
                ve.tensor_tensor(cols[:, 13:14], cols[:, 13:14], cols[:, 12:13],
                                 op=ALU.add)
                ve.drain()
                ve.tensor_scalar(cols[:, 13:14], cols[:, 13:14], float(TRASH), None,
                                 op0=ALU.min)
                ve.drain()
                vs.inc(ve.tensor_copy(dst_i[:, i:i + 1], cols[:, 13:14]))
            blk.vector(_)

            last_ts = rank_done
            last_ss = sig_done
            last_st = st.n

        def _(sy, n=vs.n):
            sy.wait_ge(vs.h, n)
            st.inc(sy.dma_start(out=dest_h[:, :], in_=dst_i[:, :]), 16)
            st.inc(sy.dma_start(out=score_h[:, :], in_=sc_s[:, :]), 16)
            sy.wait_ge(st.h, st.n)
        blk.sync(_)
    es.close()
    return nc


def build_l2():
    nc = bass.Bass("TRN2", target_bir_lowering=False, debug=False, detect_race_conditions=False)
    a2_h = nc.dram_tensor("a2", [2, 64, 64], F32, kind="ExternalInput").ap()
    b2_h = nc.dram_tensor("b2", [2, 64, 64], F32, kind="ExternalInput").ap()
    x2w_h = nc.dram_tensor("x2w", [NW, 128, 4 * W], F32, kind="ExternalInput").ap()
    st_h = nc.dram_tensor("states", [NW, 128, 4 * W], F32, kind="ExternalOutput").ap()

    NB = 4
    NG = T // 128

    es = ExitStack()
    _E = es.enter_context
    Ablk = _E(nc.sbuf_tensor("Ablk_sb", [128, 128], F32))
    Bblk = _E(nc.sbuf_tensor("Bblk_sb", [128, 128], F32))
    ascr = _E(nc.sbuf_tensor("ascr_sb", [128, 64], F32))
    x2w = _E(nc.sbuf_tensor("x2w_sb", [128, 2, 4 * W], F32))
    stw = _E(nc.sbuf_tensor("stw_sb", [128, 2, 4 * W], F32))
    st0 = _E(nc.sbuf_tensor("st0_sb", [128, 4], F32))
    psU0 = _E(nc.psum_tensor("psU0", [128, 512], F32))
    psU1 = _E(nc.psum_tensor("psU1", [128, 512], F32))
    psU2 = _E(nc.psum_tensor("psU2", [128, 512], F32))
    psU3 = _E(nc.psum_tensor("psU3", [128, 512], F32))
    _ld = _E(nc.semaphore("ld"))
    _vs = _E(nc.semaphore("vs"))
    _uS = _E(nc.semaphore("uS"))
    _mm = _E(nc.semaphore("mmS"))
    _th = _E(nc.semaphore("tanhS"))
    _st = _E(nc.semaphore("stS"))
    blk = _E(nc.Block())
    if True:
        psU = [psU0, psU1, psU2, psU3]

        @blk.vector
        def _(ve):
            ve.memset(st0[:], 0.0)
            ve.memset(Ablk[:], 0.0)
            ve.memset(Bblk[:], 0.0).then_inc(_vs, 1)

        @blk.sync
        def _(sy):
            sy.wait_ge(_vs, 1)
            sy.dma_start(out=ascr[0:64, :], in_=a2_h[0]).then_inc(_ld, 16)
            sy.dma_start(out=ascr[64:128, :], in_=a2_h[1]).then_inc(_ld, 16)
            sy.dma_start(out=Bblk[0:64, 0:64], in_=b2_h[0]).then_inc(_ld, 16)
            sy.dma_start(out=Bblk[64:128, 64:128], in_=b2_h[1]).then_inc(_ld, 16)
            sy.dma_start(out=x2w[:, 0], in_=x2w_h[0]).then_inc(_ld, 16)
            sy.dma_start(out=x2w[:, 1], in_=x2w_h[1]).then_inc(_ld, 16)
            for w in range(NW):
                if w + 2 < NW:
                    sy.wait_ge(_uS, 4 * (w + 1))
                    sy.dma_start(out=x2w[:, (w + 2) % 2],
                                 in_=x2w_h[w + 2]).then_inc(_ld, 16)
                sy.wait_ge(_th, 2 + (w + 1) * W)
                sy.dma_start(out=st_h[w], in_=stw[:, w % 2]).then_inc(_st, 16)
            sy.wait_ge(_st, 16 * NW)

        @blk.scalar
        def _(sc):
            sc.wait_ge(_ld, 32)
            sc.activation(Ablk[0:64, 0:64], ascr[0:64, :], AF.Tanh).then_inc(_th, 1)
            sc.activation(Ablk[64:128, 64:128], ascr[64:128, :],
                          AF.Tanh).then_inc(_th, 1)

        def umm(te, g):
            te.wait_ge(_ld, 64 + 16 * (g // 4 + 1))
            te.matmul(psU[g % NB][:, 0:512], Bblk[:],
                      x2w[:, (g // 4) % 2, (g % 4) * 512:(g % 4 + 1) * 512],
                      start=True, stop=False,
                      skip_group_check=True).then_inc(_uS, 1)

        @blk.tensor
        def _(te):
            te.wait_ge(_th, 2)
            te.wait_ge(_ld, 64)
            for g in range(3):
                umm(te, g)
            for t in range(T):
                g = t // 128
                if t > 0:
                    te.wait_ge(_th, 2 + t)
                if t % 128 == 0 and g + 3 < NG:
                    umm(te, g + 3)
                prev = st0[:, :] if t == 0 else \
                    stw[:, ((t - 1) // W) % 2,
                        ((t - 1) % W) * 4:((t - 1) % W) * 4 + 4]
                te.matmul(psU[g % NB][:, (t % 128) * 4:(t % 128) * 4 + 4],
                          Ablk[:], prev,
                          start=False, stop=(t % 128 == 127),
                          skip_group_check=True).then_inc(_mm, 1)
            te.wait_ge(_th, 2 + T)

        @blk.scalar
        def _(sc):
            for t in range(T):
                w = t // W
                if t % W == 0 and w >= 2:
                    sc.wait_ge(_st, 16 * (w - 1))
                sc.wait_ge(_mm, t + 1)
                sc.activation(stw[:, w % 2, (t % W) * 4:(t % W) * 4 + 4],
                              psU[(t // 128) % NB][:, (t % 128) * 4:(t % 128) * 4 + 4],
                              AF.Tanh).then_inc(_th, 1)

    es.close()
    return nc


def build_l3():
    nc = bass.Bass("TRN2", target_bir_lowering=False, debug=False, detect_race_conditions=False)
    xn_h = nc.dram_tensor("xn", [TPC, DM], F32, kind="ExternalInput").ap()
    score_h = nc.dram_tensor("score", [128, NTILE], F32, kind="ExternalInput").ap()
    xnT_h = nc.dram_tensor("xnT", [8, 128, TPC], F32, kind="ExternalInput").ap()
    x2T_h = nc.dram_tensor("x2T", [8, 128, TPC], F32, kind="ExternalInput").ap()
    x_h = nc.dram_tensor("x", [TPC, DM], F32, kind="ExternalInput").ap()
    stT_h = nc.dram_tensor("stT", [8, 128, TPC], F32, kind="ExternalInput").ap()
    dest_h = nc.dram_tensor("dest", [128, NTILE], I32, kind="ExternalInput").ap()
    tokid_h = nc.dram_tensor("tokid", [128, NTILE], I32, kind="ExternalInput").ap()
    gup_h = nc.dram_tensor("gup", [E, DM, 2 * FF], F32, kind="ExternalInput").ap()
    dwn_h = nc.dram_tensor("dwn", [E, FF, DM], F32, kind="ExternalInput").ap()
    wgT_h = nc.dram_tensor("wgT", [DM, FF], F32, kind="ExternalInput").ap()
    wuT_h = nc.dram_tensor("wuT", [DM, FF], F32, kind="ExternalInput").ap()
    wdT_h = nc.dram_tensor("wdT", [FF, DM], F32, kind="ExternalInput").ap()
    cT_h = nc.dram_tensor("cT", [H, D, S], F32, kind="ExternalInput").ap()
    woT_h = nc.dram_tensor("woT", [DM, DM], F32, kind="ExternalInput").ap()
    skip_h = nc.dram_tensor("skipc", [128, 8], F32, kind="ExternalInput").ap()
    bout_h = nc.dram_tensor("bout", [1, DM], F32, kind="ExternalInput").ap()
    ident_h = nc.dram_tensor("ident", [128, 128], F32, kind="ExternalInput").ap()
    ones_h = nc.dram_tensor("onesc", [128, 128], F32, kind="ExternalInput").ap()

    out_h = nc.dram_tensor("out", [TPC, DM], F32, kind="ExternalOutput").ap()

    xcomp = nc.dram_tensor("xcomp", [NSLOT + 1, DM], F32)
    y_hbm = nc.dram_tensor("y_hbm", [NSLOT + 1, DM], F32)
    v_hbm = nc.dram_tensor("v_hbm", [DM, DM], F32)

    es0 = ExitStack()
    _E = es0.enter_context
    ident = _E(nc.sbuf_tensor("ident_sb", [128, 128], F32))
    onesc = _E(nc.sbuf_tensor("onesc_sb", [128, 128], F32))
    dst_i = _E(nc.sbuf_tensor("dst_i_sb", [128, NTILE], I32))
    tokid = _E(nc.sbuf_tensor("tokid_sb", [128, NTILE], I32))
    scsb = _E(nc.sbuf_tensor("sc_sb", [128, NTILE], F32))
    bout = _E(nc.sbuf_tensor("bout_sb", [1, DM], F32))
    zrow = _E(nc.sbuf_tensor("zrow_sb", [1, DM], F32))
    _ld = _E(nc.semaphore("ld"))
    _vs = _E(nc.semaphore("vs"))
    _ss = _E(nc.semaphore("ss"))
    _ts = _E(nc.semaphore("ts"))
    _gp = _E(nc.semaphore("gp"))
    _st = _E(nc.semaphore("st"))
    _gw = _E(nc.semaphore("gw"))
    _gf = _E(nc.semaphore("gfree"))
    _dw = _E(nc.semaphore("dw"))
    _df = _E(nc.semaphore("dfree"))
    blk = _E(nc.Block())
    if True:
        ld, vs, ss, ts, gp, st = (Sem(_ld), Sem(_vs), Sem(_ss), Sem(_ts),
                                  Sem(_gp), Sem(_st))

        def _(sy):
            for dst, src in [(ident, ident_h), (onesc, ones_h)]:
                ld.inc(sy.dma_start(out=dst[:], in_=src[:, :]), 16)
            ld.inc(sy.dma_start(out=dst_i[:], in_=dest_h[:, :]), 16)
            ld.inc(sy.dma_start(out=tokid[:], in_=tokid_h[:, :]), 16)
            ld.inc(sy.dma_start(out=scsb[:], in_=score_h[:, :]), 16)
            ld.inc(sy.dma_start(out=bout[:], in_=bout_h[:, :]), 16)
        blk.sync(_)
        setup_ld = ld.n

        def _(ve):
            vs.inc(ve.memset(zrow[:], 0.0))
        blk.vector(_)
        zrow_done = vs.n

        with (
            nc.sbuf_tensor("xst_sb", [128, 2, DM], F32) as xst,
            nc.sbuf_tensor("zb_sb", [128, DM], F32) as zb,
        ):
            def _(ve):
                vs.inc(ve.memset(zb[:, :], 0.0))
            blk.vector(_)
            zb_done = vs.n

            def _(g, n=setup_ld, z=zrow_done, zb_n=zb_done):
                g.wait_ge(ld.h, n)
                g.wait_ge(vs.h, z)
                g.wait_ge(vs.h, zb_n)
                for r in range(0, NSLOT + 1, 128):
                    nr = min(128, NSLOT + 1 - r)
                    gp.inc(g.dma_start(out=xcomp[r:r + nr, :],
                                       in_=zb[0:nr, :]), 16)
                gp.inc(g.dma_start(out=y_hbm[TRASH:TRASH + 1, :],
                                   in_=zrow[:, :]), 16)
                g.wait_ge(gp.h, gp.n)
            blk.gpsimd(_)
            xz_done = gp.n

            xs_rd = {}
            for i in range(NTILE):
                def _(sy, i=i, w=xs_rd.get(i - 2, 0)):
                    if w:
                        sy.wait_ge(gp.h, w)
                    ld.inc(sy.dma_start(out=xst[:, i % 2],
                                        in_=xn_h[i * 128:(i + 1) * 128, :]), 16)
                blk.sync(_)
                t_ld = ld.n

                def _(ve, i=i, n=t_ld):
                    ve.wait_ge(ld.h, n)
                    vs.inc(ve.tensor_scalar(xst[:, i % 2], xst[:, i % 2],
                                            scsb[:, i:i + 1], None, op0=ALU.mult))
                blk.vector(_)
                xs_done2 = vs.n

                def _(g, i=i, n=xs_done2, zn=xz_done):
                    g.wait_ge(vs.h, n)
                    g.wait_ge(gp.h, zn)
                    gp.inc(g.indirect_dma_start(
                        out=xcomp[:, :],
                        out_offset=bass.IndirectOffsetOnAxis(dst_i[:, i:i + 1], 0),
                        in_=xst[:, i % 2], in_offset=None), 16)
                blk.gpsimd(_)
                xs_rd[i] = gp.n
        srctok_done = gp.n

        es1 = ExitStack()
        _E = es1.enter_context
        Xs = _E(nc.sbuf_tensor("Xs_sb", [128, 2, DM], F32))
        XcT = _E(nc.sbuf_tensor("XcT_sb", [128, 8, CAP], F32))
        Gbuf = _E(nc.sbuf_tensor("Gbuf_sb", [128, 2, 16, 512], F32))
        Dbuf = _E(nc.sbuf_tensor("Dbuf_sb", [128, 16, 512], F32))
        he = _E(nc.sbuf_tensor("he_sb", [128, 2, FF], F32))
        hT = _E(nc.sbuf_tensor("hT_sb", [128, 16, CAP], F32))
        ysb = _E(nc.sbuf_tensor("ysb_sb", [128, 2, DM], F32))
        silu_t = _E(nc.sbuf_tensor("silu_t_sb", [128, 512], F32))
        psA = _E(nc.psum_tensor("psA", [128, 512], F32))
        psB = _E(nc.psum_tensor("psB", [128, 512], F32))
        psDn = _E(nc.psum_tensor("psDn", [128, 512], F32))
        psTr0 = _E(nc.psum_tensor("psTr0", [128, 512], F32))
        psTr1 = _E(nc.psum_tensor("psTr1", [128, 512], F32))
        if True:
            psTr = [psTr0, psTr1]
            trc = []

            def _(sy):
                for e in range(E):
                    for j in range(4):
                        pj = e * 4 + j
                        if pj >= 2:
                            sy.wait_ge(_gf, pj - 1)
                        for dk in range(8):
                            sy.dma_start(out=Gbuf[:, pj % 2, dk, :],
                                         in_=gup_h[e, dk * 128:(dk + 1) * 128,
                                                   j * 512:(j + 1) * 512]
                                         ).then_inc(_gw, 16)
                        for dk in range(8):
                            sy.dma_start(
                                out=Gbuf[:, pj % 2, 8 + dk, :],
                                in_=gup_h[e, dk * 128:(dk + 1) * 128,
                                          FF + j * 512:FF + (j + 1) * 512]
                            ).then_inc(_gw, 16)
            blk.sync(_)

            gf_cnt = [0]
            df_cnt = [0]
            he_free_guard = [0]
            xg_guard = [0, 0]
            he_guard = {}
            ysb_guard = 0

            for e in range(E if L3_PH >= 2 else 0):
                for sg in range(2):
                    def _(g, e=e, sg=sg, n=srctok_done, w=xg_guard[sg]):
                        g.wait_ge(gp.h, n)
                        if w:
                            g.wait_ge(ts.h, w)
                        gp.inc(g.dma_start(
                            out=Xs[:, sg],
                            in_=xcomp[e * CAP + sg * 128:
                                      e * CAP + (sg + 1) * 128, :]), 16)
                    blk.gpsimd(_)
                    xs_done = gp.n

                    for c in range(8):
                        slot = len(trc) % 2
                        guard = trc[-2] if len(trc) >= 2 else 0

                        def _(te, sg=sg, c=c, slot=slot, guard=guard, n=xs_done):
                            te.wait_ge(gp.h, n)
                            if guard:
                                te.wait_ge(vs.h, guard)
                            ts.inc(te.transpose(psTr[slot][:, 0:128],
                                                Xs[:, sg, c * 128:(c + 1) * 128],
                                                ident[:]))
                        blk.tensor(_)
                        tr_n = ts.n

                        def _(ve, sg=sg, c=c, slot=slot, n=tr_n):
                            ve.wait_ge(ts.h, n)
                            vs.inc(ve.tensor_copy(
                                XcT[:, c, sg * 128:(sg + 1) * 128],
                                psTr[slot][:, 0:128]))
                        blk.vector(_)
                        trc.append(vs.n)
                    xg_guard[sg] = ts.n
                xct_ready = trc[-1]

                def _(g, e=e):
                    for nh in range(2):
                        q = e * 2 + nh
                        if q >= 1:
                            g.wait_ge(_df, q)
                        for fk in range(16):
                            g.dma_start(
                                out=Dbuf[:, fk, :],
                                in_=dwn_h[e, fk * 128:(fk + 1) * 128,
                                          nh * 512:(nh + 1) * 512]
                            ).then_inc(_dw, 16)
                blk.gpsimd(_)

                silu_rd = 0
                he_rd = {}
                for j in range(4):
                    pj = e * 4 + j
                    for sg in range(2):
                        def _(te, pj=pj, j=j, sg=sg, n=xct_ready,
                              sr=silu_rd,
                              hr=he_rd.get((j, sg), 0)):
                            te.wait_ge(vs.h, n)
                            te.wait_ge(_gw, 256 * (pj + 1))
                            if sr:
                                te.wait_ge(ss.h, sr)
                            if hr:
                                te.wait_ge(vs.h, hr)
                            for dk in range(8):
                                mm = te.matmul(psA[:, 0:512],
                                               XcT[:, dk, sg * 128:(sg + 1) * 128],
                                               Gbuf[:, pj % 2, dk, :],
                                               start=(dk == 0), stop=(dk == 7))
                            mm.then_inc(_ts, 1)
                            for dk in range(8):
                                mm = te.matmul(psB[:, 0:512],
                                               XcT[:, dk, sg * 128:(sg + 1) * 128],
                                               Gbuf[:, pj % 2, 8 + dk, :],
                                               start=(dk == 0), stop=(dk == 7))
                            mm.then_inc(_gf if sg == 1 else _ts, 1)
                        blk.tensor(_)
                        ts.n += 1
                        gu_mm = ts.n
                        if sg == 0:
                            ts.n += 1
                            psb_wait = ("ts", ts.n)
                        else:
                            gf_cnt[0] += 1
                            psb_wait = ("gf", gf_cnt[0])

                        def _(sc, n=gu_mm, hf=he_free_guard[0]):
                            sc.wait_ge(ts.h, n)
                            if hf:
                                sc.wait_ge(vs.h, hf)
                            ss.inc(sc.activation(silu_t[:, :], psA[:, 0:512],
                                                 AF.Silu))
                        blk.scalar(_)
                        silu_rd = ss.n

                        def _(ve, sg=sg, j=j, n=silu_rd,
                              hg=he_guard.get((sg, j), 0), pw=psb_wait):
                            ve.wait_ge(ss.h, n)
                            if hg:
                                ve.wait_ge(ts.h, hg)
                            ve.wait_ge(ts.h if pw[0] == "ts" else _gf, pw[1])
                            vs.inc(ve.tensor_tensor(
                                he[:, sg, j * 512:(j + 1) * 512],
                                silu_t[:, :], psB[:, 0:512], op=ALU.mult))
                        blk.vector(_)
                        he_rd[(j, sg)] = vs.n
                        he_free_guard[0] = vs.n
                he_all = vs.n

                for sg in range(2):
                    for fk in range(16):
                        slot = len(trc) % 2
                        guard = trc[-2] if len(trc) >= 2 else 0

                        def _(te, sg=sg, fk=fk, slot=slot, guard=guard,
                              n=he_all):
                            te.wait_ge(vs.h, n)
                            if guard:
                                te.wait_ge(vs.h, guard)
                            ts.inc(te.transpose(
                                psTr[slot][:, 0:128],
                                he[:, sg, fk * 128:(fk + 1) * 128], ident[:]))
                        blk.tensor(_)
                        tr_n = ts.n

                        def _(ve, sg=sg, fk=fk, slot=slot, n=tr_n):
                            ve.wait_ge(ts.h, n)
                            vs.inc(ve.tensor_copy(
                                hT[:, fk, sg * 128:(sg + 1) * 128],
                                psTr[slot][:, 0:128]))
                        blk.vector(_)
                        trc.append(vs.n)
                ht_ready = trc[-1]
                for sg in range(2):
                    for j in range(4):
                        he_guard[(sg, j)] = ts.n

                dn_rd = 0
                for nh in range(2):
                    q = e * 2 + nh
                    for sg in range(2):
                        def _(te, q=q, nh=nh, sg=sg, n=ht_ready, dr=dn_rd,
                              yg=ysb_guard):
                            te.wait_ge(vs.h, n)
                            te.wait_ge(_dw, 256 * (q + 1))
                            if dr:
                                te.wait_ge(vs.h, dr)
                            if yg and nh == 0 and sg == 0:
                                te.wait_ge(gp.h, yg)
                            for fk in range(16):
                                mm = te.matmul(psDn[:, 0:512],
                                               hT[:, fk, sg * 128:(sg + 1) * 128],
                                               Dbuf[:, fk, :],
                                               start=(fk == 0), stop=(fk == 15))
                            mm.then_inc(_df if sg == 1 else _ts, 1)
                        blk.tensor(_)
                        if sg == 0:
                            ts.n += 1
                            dn_wait = ("ts", ts.n)
                        else:
                            df_cnt[0] += 1
                            dn_wait = ("df", df_cnt[0])

                        def _(ve, nh=nh, sg=sg, w=dn_wait):
                            ve.wait_ge(ts.h if w[0] == "ts" else _df, w[1])
                            vs.inc(ve.tensor_copy(
                                ysb[:, sg, nh * 512:(nh + 1) * 512],
                                psDn[:, 0:512]))
                        blk.vector(_)
                        dn_rd = vs.n
                y_done = vs.n

                def _(g, e=e, n=y_done):
                    g.wait_ge(vs.h, n)
                    for sg in range(2):
                        gp.inc(g.dma_start(
                            out=y_hbm[e * CAP + sg * 128:
                                      e * CAP + (sg + 1) * 128, :],
                            in_=ysb[:, sg]), 16)
                blk.gpsimd(_)
                ysb_guard = gp.n
            moe_ts = ts.n
            moe_vs = vs.n
            moe_gp = gp.n
        es1.close()

        es2 = ExitStack()
        _E = es2.enter_context
        W2T = _E(nc.sbuf_tensor("W2T_sb", [128, 8, DM], F32))
        Vsb = _E(nc.sbuf_tensor("Vsb_sb", [128, 8, DM], F32))
        es2b = ExitStack()
        _E = es2b.enter_context
        cTs = _E(nc.sbuf_tensor("cTs_sb", [64, H, S], F32))
        wo64 = _E(nc.sbuf_tensor("wo64_sb", [64, H, DM], F32))
        wo128 = _E(nc.sbuf_tensor("wo128_sb", [128, 8, DM], F32))
        skipc = _E(nc.sbuf_tensor("skipc_sb", [128, 8], F32))
        vtmp = _E(nc.sbuf_tensor("vtmp_sb", [64, 512], F32))
        psV = _E(nc.psum_tensor("psV", [128, 512], F32))
        if True:
            def _(sy, m=moe_vs, mt=moe_ts, mg=moe_gp):
                sy.wait_ge(vs.h, m)
                sy.wait_ge(ts.h, mt)
                sy.wait_ge(gp.h, mg)
                ld.inc(sy.dma_start(out=cTs[:, :, :],
                                    in_=cT_h.rearrange("h d s -> d h s")), 16)
                ld.inc(sy.dma_start(out=wo64[:, :, :],
                                    in_=woT_h.rearrange("(h p) o -> p h o", p=64)),
                       16)
                ld.inc(sy.dma_start(out=wo128[:, :, :],
                                    in_=woT_h.rearrange("(c p) o -> p c o", p=128)),
                       16)
                ld.inc(sy.dma_start(out=skipc[:, :], in_=skip_h[:, :]), 16)
            blk.sync(_)
            vw_ld = ld.n

            def _(ve, n=vw_ld):
                ve.wait_ge(ld.h, n)
                for c in range(8):
                    ii = ve.tensor_scalar(W2T[:, c], wo128[:, c], skipc[:, c:c + 1],
                                          None, op0=ALU.mult)
                vs.inc(ii)
            blk.vector(_)
            w2t_done = vs.n

            vcopy = 0
            for h in range(H if L3_PH >= 3 else 0):
                for no in range(2):
                    def _(te, h=h, no=no, n=vw_ld, m=vcopy, g0=gp.n):
                        te.wait_ge(ld.h, n)
                        if m:
                            te.wait_ge(vs.h, m)
                            te.wait_ge(gp.h, g0)
                        ts.inc(te.matmul(psV[0:64, 0:512], cTs[:, h, :],
                                         wo64[:, h, no * 512:(no + 1) * 512],
                                         start=True, stop=True))
                    blk.tensor(_)
                    v_mm = ts.n

                    def _(ve, n=v_mm):
                        ve.wait_ge(ts.h, n)
                        vs.inc(ve.tensor_copy(vtmp[:, :], psV[0:64, 0:512]))
                    blk.vector(_)
                    vcopy = vs.n

                    def _(g, h=h, no=no, n=vcopy):
                        g.wait_ge(vs.h, n)
                        gp.inc(g.dma_start(
                            out=v_hbm[h * 64:(h + 1) * 64,
                                      no * 512:(no + 1) * 512],
                            in_=vtmp[:, :]), 16)
                    blk.gpsimd(_)
            v_stores = gp.n

            def _(sy, n=v_stores):
                sy.wait_ge(gp.h, n)
                ld.inc(sy.dma_start(out=Vsb[:, :, :],
                                    in_=v_hbm.ap().rearrange("(c p) o -> p c o",
                                                             p=128)), 16)
            blk.sync(_)
            v_ready = ld.n
            es2b.close()

            es3 = ExitStack()
            _E = es3.enter_context
            outacc = _E(nc.sbuf_tensor("outacc_sb", [128, NTILE, DM], F32))
            es3b = ExitStack()
            _E = es3b.enter_context
            wgb = _E(nc.sbuf_tensor("wgb_sb", [128, 8, 512], F32))
            wub = _E(nc.sbuf_tensor("wub_sb", [128, 8, 512], F32))
            wdb = _E(nc.sbuf_tensor("wdb_sb", [128, 4, DM], F32))
            xnTt = _E(nc.sbuf_tensor("xnTt_sb", [128, 8, 128], F32))
            hsh = _E(nc.sbuf_tensor("hsh_sb", [128, 512], F32))
            hshT = _E(nc.sbuf_tensor("hshT_sb", [128, 4, 128], F32))
            silu2 = _E(nc.sbuf_tensor("silu2_sb", [128, 512], F32))
            psG2 = _E(nc.psum_tensor("psG2", [128, 512], F32))
            psU2b = _E(nc.psum_tensor("psU2", [128, 512], F32))
            psD2 = _E(nc.psum_tensor("psD2", [128, 512], F32))
            psTa = _E(nc.psum_tensor("psTa", [128, 512], F32))
            psTb = _E(nc.psum_tensor("psTb", [128, 512], F32))
            if True:
                psT2 = [psTa, psTb]

                def _(ve, m=moe_vs, mt=moe_ts, mg=moe_gp):
                    ve.wait_ge(ts.h, mt)
                    ve.wait_ge(gp.h, mg)
                    vs.inc(ve.memset(outacc[:, :, :], 0.0))
                blk.vector(_)
                oz = vs.n

                wld = 0
                trc2 = []
                silu2_rd = 0
                hsh_rd = 0
                hsh_rd0 = [0]
                acc_wr = oz
                for ng in range(4 if L3_PH >= 4 else 0):
                    def _(sy, ng=ng, w=wld, mt=moe_ts, mg=moe_gp, mv=moe_vs):
                        if ng == 0 and not w:
                            sy.wait_ge(ts.h, mt)
                            sy.wait_ge(gp.h, mg)
                            sy.wait_ge(vs.h, mv)
                        if w:
                            sy.wait_ge(ts.h, w)
                        for dk in range(8):
                            ld.inc(sy.dma_start(
                                out=wgb[:, dk, :],
                                in_=wgT_h[dk * 128:(dk + 1) * 128,
                                          ng * 512:(ng + 1) * 512]), 16)
                        for dk in range(8):
                            ld.inc(sy.dma_start(
                                out=wub[:, dk, :],
                                in_=wuT_h[dk * 128:(dk + 1) * 128,
                                          ng * 512:(ng + 1) * 512]), 16)
                        for fk in range(4):
                            ld.inc(sy.dma_start(
                                out=wdb[:, fk, :],
                                in_=wdT_h[ng * 512 + fk * 128:
                                          ng * 512 + (fk + 1) * 128, :]), 16)
                    blk.sync(_)
                    ng_ld = ld.n

                    for i in range(NTILE):
                        def _(sy, ng=ng, i=i, w=acc_wr):
                            sy.wait_ge(vs.h, w)
                            ld.inc(sy.dma_start(
                                out=xnTt[:, :, :],
                                in_=xnT_h[:, :, i * 128:(i + 1) * 128]
                                .rearrange("c p t -> p c t")), 16)
                        blk.sync(_)
                        t_ld = ld.n

                        def _(te, ng=ng, i=i, n=t_ld, m=ng_ld, sr=silu2_rd,
                              hr=hsh_rd):
                            te.wait_ge(ld.h, max(n, m))
                            if sr:
                                te.wait_ge(ss.h, sr)
                            if hr:
                                te.wait_ge(vs.h, hr)
                            for dk in range(8):
                                mm = te.matmul(psG2[:, 0:512], xnTt[:, dk],
                                               wgb[:, dk],
                                               start=(dk == 0), stop=(dk == 7))
                            ts.inc(mm)
                            for dk in range(8):
                                mm = te.matmul(psU2b[:, 0:512], xnTt[:, dk],
                                               wub[:, dk],
                                               start=(dk == 0), stop=(dk == 7))
                            ts.inc(mm)
                        blk.tensor(_)
                        gu2 = ts.n - 1

                        def _(sc, n=gu2, hf=hsh_rd0[0]):
                            sc.wait_ge(ts.h, n)
                            if hf:
                                sc.wait_ge(vs.h, hf)
                            ss.inc(sc.activation(silu2[:, :], psG2[:, 0:512],
                                                 AF.Silu))
                        blk.scalar(_)
                        silu2_rd = ss.n

                        def _(ve, n=silu2_rd, m=ts.n):
                            ve.wait_ge(ss.h, n)
                            ve.wait_ge(ts.h, m)
                            vs.inc(ve.tensor_tensor(hsh[:, :], silu2[:, :],
                                                    psU2b[:, 0:512], op=ALU.mult))
                        blk.vector(_)
                        hsh_done = vs.n
                        hsh_rd0[0] = vs.n

                        for fk in range(4):
                            slot = len(trc2) % 2
                            guard = trc2[-2] if len(trc2) >= 2 else 0

                            def _(te, fk=fk, slot=slot, guard=guard, n=hsh_done):
                                te.wait_ge(vs.h, n)
                                if guard:
                                    te.wait_ge(vs.h, guard)
                                ts.inc(te.transpose(
                                    psT2[slot][:, 0:128],
                                    hsh[:, fk * 128:(fk + 1) * 128], ident[:]))
                            blk.tensor(_)
                            tr2 = ts.n

                            def _(ve, fk=fk, slot=slot, n=tr2):
                                ve.wait_ge(ts.h, n)
                                vs.inc(ve.tensor_copy(hshT[:, fk],
                                                      psT2[slot][:, 0:128]))
                            blk.vector(_)
                            trc2.append(vs.n)
                        hsh_rd = trc2[-1]

                        for nh in range(2):
                            def _(te, nh=nh, n=hsh_rd, w=acc_wr):
                                te.wait_ge(vs.h, max(n, w))
                                for fk in range(4):
                                    mm = te.matmul(psD2[:, 0:512], hshT[:, fk],
                                                   wdb[:, fk,
                                                       nh * 512:(nh + 1) * 512],
                                                   start=(fk == 0), stop=(fk == 3))
                                ts.inc(mm)
                            blk.tensor(_)
                            d2 = ts.n

                            def _(ve, i=i, nh=nh, n=d2):
                                ve.wait_ge(ts.h, n)
                                vs.inc(ve.tensor_tensor(
                                    outacc[:, i, nh * 512:(nh + 1) * 512],
                                    outacc[:, i, nh * 512:(nh + 1) * 512],
                                    psD2[:, 0:512], op=ALU.add))
                            blk.vector(_)
                            acc_wr = vs.n
                    wld = ts.n
                shared_done = vs.n
                es3b.close()

                es4 = ExitStack()
                _E = es4.enter_context
                stTt = _E(nc.sbuf_tensor("stTt_sb", [128, 8, 128], F32))
                x2Tt = _E(nc.sbuf_tensor("x2Tt_sb", [128, 8, 128], F32))
                xnt2 = _E(nc.sbuf_tensor("xnt2_sb", [128, DM], F32))
                xt2 = _E(nc.sbuf_tensor("xt2_sb", [128, DM], F32))
                yg = _E(nc.sbuf_tensor("yg_sb", [128, DM], F32))
                gsb = _E(nc.sbuf_tensor("gsb_sb", [128, DM], F32))
                ssb = _E(nc.sbuf_tensor("ssb_sb", [128, DM], F32))
                osb = _E(nc.sbuf_tensor("osb_sb", [128, DM], F32))
                psS0 = _E(nc.psum_tensor("psS0", [128, 512], F32))
                psS1 = _E(nc.psum_tensor("psS1", [128, 512], F32))
                if True:
                    psS = [psS0, psS1]
                    ep_prev = 0
                    ep_store = 0
                    if L3_PH < 5:
                        def _(g):
                            for i in range(NTILE):
                                gp.inc(g.dma_start(
                                    out=out_h[i * 128:(i + 1) * 128, :],
                                    in_=yg[:, :]), 16)
                            g.wait_ge(gp.h, gp.n)
                        blk.gpsimd(_)
                    for i in range(NTILE if L3_PH >= 5 else 0):
                        def _(sy, i=i, n=v_ready, w=ep_prev, sd=shared_done,
                              st_=ts.n):
                            sy.wait_ge(ld.h, n)
                            if i == 0:
                                sy.wait_ge(vs.h, sd)
                                sy.wait_ge(ts.h, st_)
                            if w:
                                sy.wait_ge(vs.h, w)
                            ld.inc(sy.dma_start(
                                out=stTt[:, :, :],
                                in_=stT_h[:, :, i * 128:(i + 1) * 128]
                                .rearrange("c p t -> p c t")), 16)
                            ld.inc(sy.dma_start(
                                out=x2Tt[:, :, :],
                                in_=x2T_h[:, :, i * 128:(i + 1) * 128]
                                .rearrange("c p t -> p c t")), 16)
                            ld.inc(sy.dma_start(
                                out=xnt2[:, :],
                                in_=xn_h[i * 128:(i + 1) * 128, :]), 16)
                            ld.inc(sy.dma_start(
                                out=xt2[:, :],
                                in_=x_h[i * 128:(i + 1) * 128, :]), 16)
                        blk.sync(_)
                        ep_ld = ld.n

                        def _(g, i=i, n=shared_done):
                            g.wait_ge(vs.h, n)
                            gp.inc(g.indirect_dma_start(
                                out=yg[:, :], out_offset=None,
                                in_=y_hbm[:, :],
                                in_offset=bass.IndirectOffsetOnAxis(
                                    dst_i[:, i:i + 1], 0)), 16)
                        blk.gpsimd(_)
                        yg_ld = gp.n

                        def _(te, n=ep_ld, w=ep_prev, m=w2t_done):
                            te.wait_ge(ld.h, n)
                            te.wait_ge(vs.h, m)
                            if w:
                                te.wait_ge(vs.h, w)
                            for no in range(2):
                                for c in range(8):
                                    te.matmul(psS[no][:, 0:512], stTt[:, c],
                                              Vsb[:, c, no * 512:(no + 1) * 512],
                                              start=(c == 0), stop=False)
                                for c in range(8):
                                    te.matmul(psS[no][:, 0:512], x2Tt[:, c],
                                              W2T[:, c, no * 512:(no + 1) * 512],
                                              start=False, stop=False)
                                mm = te.matmul(psS[no][:, 0:512], onesc[0:1, :],
                                               bout[:, no * 512:(no + 1) * 512],
                                               start=False, stop=True)
                            ts.inc(mm)
                        blk.tensor(_)
                        s_mm = ts.n

                        def _(ve, i=i, n=s_mm, m=yg_ld):
                            ve.wait_ge(ts.h, n)
                            ve.wait_ge(gp.h, m)
                            for no in range(2):
                                ve.tensor_tensor(
                                    ssb[:, no * 512:(no + 1) * 512],
                                    psS[no][:, 0:512],
                                    xnt2[:, no * 512:(no + 1) * 512], op=ALU.add)
                            vs.inc(ve.tensor_tensor(gsb[:, :], outacc[:, i, :],
                                                    yg[:, :], op=ALU.add))
                        blk.vector(_)
                        gpre_done = vs.n

                        def _(sc, n=gpre_done):
                            sc.wait_ge(vs.h, n)
                            ss.inc(sc.activation(gsb[:, :], gsb[:, :], AF.Silu))
                        blk.scalar(_)
                        g_done = ss.n

                        def _(ve, n=g_done, w=ep_store if i > 0 else 0):
                            ve.wait_ge(ss.h, n)
                            if w:
                                ve.wait_ge(gp.h, w)
                            ve.drain()
                            ve.tensor_tensor(gsb[:, :], ssb[:, :], gsb[:, :],
                                             op=ALU.mult)
                            ve.drain()
                            vs.inc(ve.tensor_tensor(osb[:, :], xt2[:, :],
                                                    gsb[:, :], op=ALU.add))
                        blk.vector(_)
                        o_done = vs.n

                        def _(g, i=i, n=o_done):
                            g.wait_ge(vs.h, n)
                            gp.inc(g.dma_start(
                                out=out_h[i * 128:(i + 1) * 128, :],
                                in_=osb[:, :]), 16)
                        blk.gpsimd(_)
                        ep_prev = o_done
                        ep_store = gp.n

                    def _(g, n=gp.n):
                        g.wait_ge(gp.h, n)
                    blk.gpsimd(_)
                es4.close()
            es3.close()
        es2.close()
    es0.close()
    return nc


_NC_CACHE = {}


def _get(name, builder):
    if name not in _NC_CACHE:
        _NC_CACHE[name] = builder()
    return _NC_CACHE[name]


def kernel(L0_attn_tokens, L0_attn_norm_w, L0_attn_a, L0_attn_b, L0_attn_c,
           L0_attn_skip, L0_attn_w_out, L0_attn_b_out, L0_ffn_w_router,
           L0_ffn_gate_up_proj, L0_ffn_down_proj, L0_ffn_w_gate, L0_ffn_w_up,
           L0_ffn_w_down):
    f32 = np.float32
    x = np.ascontiguousarray(np.asarray(L0_attn_tokens, f32).reshape(NTOK, DM))
    norm_w = np.asarray(L0_attn_norm_w, f32)
    a_in = np.asarray(L0_attn_a, f32)
    b_in = np.asarray(L0_attn_b, f32)
    c_in = np.asarray(L0_attn_c, f32)
    skip = np.asarray(L0_attn_skip, f32)
    w_out = np.asarray(L0_attn_w_out, f32)
    b_out = np.asarray(L0_attn_b_out, f32)
    w_router = np.asarray(L0_ffn_w_router, f32)
    gup = np.ascontiguousarray(np.asarray(L0_ffn_gate_up_proj, f32))
    dwn = np.ascontiguousarray(np.asarray(L0_ffn_down_proj, f32))
    w_gate = np.asarray(L0_ffn_w_gate, f32)
    w_up = np.asarray(L0_ffn_w_up, f32)
    w_down = np.asarray(L0_ffn_w_down, f32)

    ident = np.eye(128, dtype=f32)
    ltri = np.tril(np.ones((128, 128), f32), -1)
    iota = np.broadcast_to(np.arange(E, dtype=f32), (128, E)).copy()
    iota99 = iota - 99.0
    onesc = np.ones((128, 128), f32)
    nwrep = np.broadcast_to(norm_w, (128, DM)).copy()
    wrt = np.ascontiguousarray(w_router.T)
    tokid = (np.arange(TPC, dtype=np.int32).reshape(NTILE, 128).T).copy()
    fillc = np.full((128, 2112), TPC, dtype=np.int32)
    wgT = np.ascontiguousarray(w_gate.T)
    wuT = np.ascontiguousarray(w_up.T)
    wdT = np.ascontiguousarray(w_down.T)
    cT = np.ascontiguousarray(np.transpose(c_in, (0, 2, 1)))
    woT = np.ascontiguousarray(w_out.T)
    skipc = np.ascontiguousarray(skip.reshape(DM).reshape(8, 128).T)
    boutr = np.ascontiguousarray(b_out.reshape(1, DM))

    nc1 = _get("l1", build_l1)
    in1 = []
    for k in range(NCORE):
        in1.append({
            "x": x[k * TPC:(k + 1) * TPC],
            "nwrep": nwrep, "wrt": wrt, "ident": ident, "ltri": ltri,
            "iota": iota, "iota99": iota99, "onesc": onesc,
        })
    r1 = run_bass_kernel_spmd(nc1, in1, core_ids=list(range(NCORE)))
    xn_all = [r1.results[k]["xn"] for k in range(NCORE)]
    xnT_all = [r1.results[k]["xnT"] for k in range(NCORE)]
    x2T_all = [r1.results[k]["x2T"] for k in range(NCORE)]
    dest_all = [r1.results[k]["dest"].T.reshape(TPC) for k in range(NCORE)]
    score_all = [r1.results[k]["score"].T.reshape(TPC) for k in range(NCORE)]

    in2 = []
    for k in range(NCORE):
        ch, po = k // 1, 0
        x2w = np.empty((NW, 128, W, B), f32)
        for w in range(NW):
            t0 = w * W
            for b in range(B):
                srcc = 2 * b + (1 if t0 >= T // 2 else 0)
                tloc = t0 - (T // 2 if t0 >= T // 2 else 0)
                x2w[w, :, :, b] = x2T_all[srcc][k, :, tloc:tloc + W].transpose(0, 1)
        x2w = np.ascontiguousarray(x2w.reshape(NW, 128, 4 * W))
        in2.append({
            "a2": a_in[2 * k:2 * k + 2],
            "b2": b_in[2 * k:2 * k + 2],
            "x2w": x2w,
        })
    nc2 = _get("l2", build_l2)
    r2 = run_bass_kernel_spmd(nc2, in2, core_ids=list(range(NCORE)))
    states_all = [r2.results[k]["states"] for k in range(NCORE)]

    in3 = []
    for k in range(NCORE):
        bk = k // 2
        th = (k % 2) * (T // 2)
        stT = np.empty((8, 128, TPC), f32)
        for m in range(8):
            sa = states_all[m]
            for w in range(T // 2 // W):
                wg = (th // W) + w
                stT[m, :, w * W:(w + 1) * W] = \
                    sa[wg].reshape(128, W, B)[:, :, bk]
        score_i = np.ascontiguousarray(
            score_all[k].reshape(NTILE, 128).T.astype(f32))
        dest_i = np.ascontiguousarray(
            dest_all[k].reshape(NTILE, 128).T.astype(np.int32))
        in3.append({
            "xn": xn_all[k], "xnT": xnT_all[k], "x2T": x2T_all[k],
            "x": x[k * TPC:(k + 1) * TPC], "stT": stT,
            "dest": dest_i, "score": score_i, "tokid": tokid,
            "gup": gup, "dwn": dwn, "wgT": wgT, "wuT": wuT, "wdT": wdT,
            "cT": cT, "woT": woT, "skipc": skipc, "bout": boutr,
            "ident": ident, "onesc": onesc,
        })
    nc3 = _get("l3", build_l3)
    r3 = run_bass_kernel_spmd(nc3, in3, core_ids=list(range(NCORE)))
    out = np.concatenate([r3.results[k]["out"] for k in range(NCORE)], axis=0)
    return out.reshape(B, T, DM)
